# version 12
# baseline (speedup 1.0000x reference)
import contextlib
import numpy as np
import ml_dtypes
import concourse.bass as bass
import concourse.mybir as mybir
from concourse.alu_op_type import AluOpType
from concourse import library_config
from concourse.bass_utils import run_bass_kernel_spmd

F32 = mybir.dt.float32
BF16 = mybir.dt.bfloat16
AF = mybir.ActivationFunctionType
AX = mybir.AxisListType
BF16NP = ml_dtypes.bfloat16

B = 4
N = 4096
D = 1024
DC = 8
H = 16
DH = 64
E = 8
FF = 4096
FC = 32
TOWN = 2048
NSLOT = 8
SL = 256
KC = 33
KVN = KC * 128
LNEPS = 1e-5
QSCALE = DH ** -0.5
ATTN_SKEW = True
ATTN_NSLOT = NSLOT

DMA_FAMS = {"ld0": "gp", "wb1a": "gp", "wb1b": "gp", "wb2a": "gp",
            "wb2b": "gp", "spi": "sp", "osca": "sp", "oscb": "sp", "spm": "sp", "st": "sp"}
ALL_ENGS = ("pe", "act", "dve", "gp") + tuple(DMA_FAMS)


class Buf:
    __slots__ = ("name", "writer", "readers")

    def __init__(self, name):
        self.name = name
        self.writer = None
        self.readers = {}

    def __repr__(self):
        return f"Buf({self.name})"


class Prog:

    def __init__(self, needed=None):
        self.ops = {e: [] for e in ("pe", "act", "dve", "gp", "sp")}
        self.cnt = {e: 0 for e in ALL_ENGS}
        self.waited = {}
        self.drained = {e: 0 for e in ("pe", "act", "dve", "gp", "sp")}
        self.n_waits = 0
        self.n_drains = 0
        self.needed = needed if needed is not None else {e: set() for e in ALL_ENGS}
        self.pass2 = needed is not None
        self.semval = {e: 0 for e in ALL_ENGS}
        self.svmap = {e: {} for e in ALL_ENGS}
        self.n_incs = 0

    @staticmethod
    def _stream(eng):
        return DMA_FAMS.get(eng, eng)

    def op(self, eng, fn, reads=(), writes=(), tag=""):
        stream = self._stream(eng)
        is_dma = eng in DMA_FAMS
        waits = {}
        need_drain = False
        for b in tuple(reads) + tuple(writes):
            w = b.writer
            if w is not None:
                weng, wcnt = w
                if weng == eng and not is_dma:
                    if eng != "pe" and wcnt > self.drained[eng]:
                        need_drain = True
                elif weng != eng:
                    v = self.cnt[weng] if weng in DMA_FAMS else wcnt
                    if waits.get(weng, 0) < v:
                        waits[weng] = v
        for b in writes:
            for reng, rcnt in b.readers.items():
                if reng == eng and not is_dma:
                    if eng != "pe" and rcnt > self.drained[eng]:
                        need_drain = True
                elif reng != eng:
                    v = self.cnt[reng] if reng in DMA_FAMS else rcnt
                    if waits.get(reng, 0) < v:
                        waits[reng] = v
        final_waits = []
        for peng, val in waits.items():
            key = (stream, peng)
            if self.waited.get(key, 0) < val:
                self.waited[key] = val
                final_waits.append((peng, val))
                if not self.pass2:
                    self.needed[peng].add(val)
        inc = 16 if is_dma else 1
        self.cnt[eng] += inc
        my = self.cnt[eng]
        self.n_waits += len(final_waits)
        if need_drain:
            self.n_drains += 1
            self.drained[eng] = my - inc

        if self.pass2:
            do_inc = is_dma or my in self.needed[eng]
            if do_inc:
                self.semval[eng] += inc
                self.svmap[eng][my] = self.semval[eng]
                self.n_incs += 1
            tw = tuple((p, v if p in DMA_FAMS else self.svmap[p][v])
                       for p, v in final_waits)
        else:
            do_inc = True
            self.semval[eng] += inc
            self.svmap[eng][my] = self.semval[eng]
            tw = tuple(final_waits)

        def thunk(e, sems, _w=tw, _d=need_drain, _fn=fn,
                  _inc=inc, _eng=eng, _di=do_inc):
            for peng, val in _w:
                e.wait_ge(sems[peng], val)
            if _d:
                e.drain()
            inst = _fn(e)
            if _di:
                inst.then_inc(sems[_eng], _inc)

        self.ops[stream].append(thunk)
        for b in writes:
            b.writer = (eng, my)
            b.readers = {}
        for b in reads:
            b.readers[eng] = my
        return my

    def raw(self, stream, fn):
        self.ops[stream].append(lambda e, sems, _fn=fn: _fn(e))

    def final_wait(self, stream, eng, val):
        if not self.pass2:
            self.needed[eng].add(val)
        v = val if eng in DMA_FAMS else self.svmap[eng].get(val, val)
        self.ops[stream].append(
            lambda e, sems, _eng=eng, _v=v: e.wait_ge(sems[_eng], _v))


def build(nc, dbg=(), stages=6, needed=None, emit=True):
    P = Prog(needed)

    xT = nc.dram_tensor("xT", [D, N], F32, kind="ExternalInput").ap()
    xTo = nc.dram_tensor("xTo", [D, TOWN], F32, kind="ExternalInput").ap()
    wq = nc.dram_tensor("wq", [D, D], F32, kind="ExternalInput").ap()
    wkv = nc.dram_tensor("wkv", [D, DH], F32, kind="ExternalInput").ap()
    nullkv = nc.dram_tensor("nullkv", [DH, 1], F32, kind="ExternalInput").ap()
    woutH = nc.dram_tensor("woutH", [DH, H * D], F32, kind="ExternalInput").ap()
    wgt = nc.dram_tensor("wgt", [D, 9], F32, kind="ExternalInput").ap()
    bgt = nc.dram_tensor("bgt", [9, 1], F32, kind="ExternalInput").ap()
    w1t = nc.dram_tensor("w1t", [E, FC, 128, 1024], F32, kind="ExternalInput").ap()
    b1t = nc.dram_tensor("b1t", [128, E * FC], F32, kind="ExternalInput").ap()
    w2t = nc.dram_tensor("w2t", [E, DC, 128, 4096], F32, kind="ExternalInput").ap()
    b2t = nc.dram_tensor("b2t", [128, E * DC], F32, kind="ExternalInput").ap()
    masks = nc.dram_tensor("masks", [128, 1280], BF16, kind="ExternalInput").ap()
    yT = nc.dram_tensor("yT", [D, TOWN], F32, kind="ExternalOutput").ap()

    dbg_aps = {}
    for name, shape, dt in (("kvf", [128, KVN], BF16),
                            ("qb", [128, 16384], BF16),
                            ("xn2", [128, 16384], BF16),
                            ("wfm", [8, 2048], F32),
                            ("xattn", [D, TOWN], F32),
                            ("kvk", [128, KC * 65], BF16),
                            ("ao", [64, 4096], BF16),
                            ("negc", [128, 1], F32),
                            ("mk", [128, 1280], BF16),
                            ("pp", [128, 2 * SL], BF16),
                            ("lrow", [1, 2 * SL], F32)):
        if name in dbg:
            dbg_aps[name] = nc.dram_tensor("dbg_" + name, shape, dt,
                                           kind="ExternalOutput").ap()

    stack = contextlib.ExitStack()
    sbt = lambda name, shape, dt: stack.enter_context(nc.sbuf_tensor(name, shape, dt))
    pst = lambda name, shape, dt: stack.enter_context(nc.psum_tensor(name, shape, dt))

    xs = [sbt(f"xs{c}", [128, TOWN], F32) for c in range(DC)]
    AR = sbt("arena", [128, 49152], BF16)
    kvf = sbt("kvf", [128, KVN], BF16)
    wb1 = [sbt(f"wb1_{i}", [128, 2048], BF16) for i in range(2)]
    wb2 = [sbt(f"wb2_{i}", [128, 2048], BF16) for i in range(2)]
    fsc = sbt("fsc", [128, 5248], F32)
    wgts = sbt("wgts", [128, 72], BF16)

    def colA_v(c, c0, n):
        return AR[:, c * 2048 + c0: c * 2048 + c0 + n]

    WQ0 = 16384
    def wq_v(k, j0, n):
        return AR[:, WQ0 + k * 1024 + j0: WQ0 + k * 1024 + j0 + n]

    def woutH_v(h, j0, n):
        return AR[0:64, WQ0 + h * 1024 + j0: WQ0 + h * 1024 + j0 + n]

    XN0 = 32768
    def xnk_v(c, n=256):
        return AR[:, XN0 + c * 256: XN0 + c * 256 + n]

    def xnq_v(c, c0, n):
        return AR[:, XN0 + c * 512 + c0: XN0 + c * 512 + c0 + n]

    AO0 = 36864
    def ao_v(h):
        return AR[0:64, AO0 + h * SL: AO0 + (h + 1) * SL]

    MK0 = 40960
    def maskp_v(j, n=512):
        return AR[:, MK0 + j * 512: MK0 + j * 512 + n]

    PB0 = 42240
    def p2_v(hf, par):
        i = 2 * hf + par
        return AR[:, PB0 + i * 512: PB0 + (i + 1) * 512]

    SQ0 = 44544
    def sqb_v(par, n=512):
        return AR[:, SQ0 + par * 512: SQ0 + par * 512 + n]

    ones_b = AR[:, 45568:45569]
    WKV0 = 45584
    def wkv_v(k):
        return AR[:, WKV0 + k * 64: WKV0 + (k + 1) * 64]

    def wgt_v(k):
        return wgts[:, k * 9:(k + 1) * 9]

    IDB0 = 46176
    identb = AR[:, IDB0:IDB0 + 128]
    KVK0 = 46336
    kvk = AR[:, KVK0:KVK0 + KC * 65 + 63]

    def hp_v(fc, c0, n):
        return AR[:, 16384 + fc * 1024 + c0: 16384 + fc * 1024 + c0 + n]

    row_m = fsc[0:1, 0:512]
    row_ms = fsc[32:33, 0:512]
    row_var = fsc[64:65, 0:512]
    row_tmp = fsc[96:97, 0:512]
    row_rstd = fsc[0:1, 512:1024]
    row_mr = fsc[32:33, 512:1024]
    wrow = fsc[96:97, 512:1024]
    row_kn2 = fsc[64:65, 512:513]
    row_sc1 = fsc[64:65, 513:514]
    row_sc2 = fsc[64:65, 514:515]
    row_linv = [fsc[64:65, 0:256], fsc[64:65, 256:512]]
    ones_f = fsc[:, 1024:1025]
    negc_cols = [fsc[:, 1025:1026], fsc[:, 1033:1034]]
    eps_col = fsc[:, 1026:1027]
    g_negmax = fsc[:, 1027:1028]
    g_zi = fsc[:, 1028:1029]
    g_thr = fsc[:, 1029:1030]
    g_s = fsc[:, 1030:1031]
    g_eq = fsc[:, 1031:1032]
    bgt_col = fsc[0:9, 1032:1033]
    g_pg = fsc[:, 1036:1044]
    g_w = fsc[:, 1044:1052]
    b1sb = fsc[:, 1056:1312]
    identf = fsc[:, 1312:1440]
    b2sb = fsc[:, 1440:1504]
    REP = (1536, 2048)
    def rep_v(i, n=512):
        return fsc[:, REP[i]:REP[i] + n]
    gsb = fsc[0:9, 1536:2048]
    T10 = 2560
    def t1_v(n=512):
        return fsc[:, T10:T10 + n]
    def t1p_v(p0, p1, n):
        return fsc[p0:p1, T10:T10 + n]
    def lrep_v(par):
        return fsc[0:64, T10 + par * SL: T10 + (par + 1) * SL]
    wfm = fsc[0:8, 3072:5120]
    def osc_v(c, c0, n):
        return fsc[:, 3072 + c * 256 + c0: 3072 + c * 256 + c0 + n]
    onesblk = fsc[:, 5120:5248]

    pb = [pst(f"pb{i}", [128, 512], F32) for i in range(7)]
    pbT = pst("pbT", [128, 1024], BF16)

    bxs = [Buf(f"xs{c}") for c in range(DC)]
    bcolA = [Buf(f"colA{c}") for c in range(DC)]
    bkvf = Buf("kvf")
    bkvk = Buf("kvk")
    bwb1 = [Buf(f"wb1_{i}") for i in range(2)]
    bwb2 = [Buf(f"wb2_{i}") for i in range(2)]
    bwq = Buf("wq_wout")
    bxn = [Buf(f"xn{c}") for c in range(DC)]
    bao = [Buf(f"ao{h}") for h in range(H)]
    bmask = Buf("mask")
    bp = [Buf(f"p{i}") for i in range(4)]
    bsqb = [Buf(f"sqb{i}") for i in range(2)]
    bonesb = Buf("onesb")
    bonesf = Buf("onesf")
    bidf = Buf("identf")
    bidb = Buf("identb")
    bwkv = Buf("wkv")
    bwgt = Buf("wgt")
    bosc = Buf("osc")
    brow = Buf("rows")
    bnegc = [Buf("negc0"), Buf("negc1")]
    blinv = [Buf(f"linv{i}") for i in range(2)]
    brep = [Buf(f"rep{i}") for i in range(2)]
    bb1 = Buf("b1")
    bb2 = Buf("b2")
    bbgt = Buf("bgt")
    bgate = Buf("gatescratch")
    bgsb = brep[0]
    bwfm = Buf("wfm")
    bhp = [Buf(f"hp{fc}") for fc in range(FC)]
    bpb = [Buf(f"pb{i}") for i in range(7)]
    bpbT = Buf("pbT")
    bkn2 = brow
    bt1 = Buf("t1")
    blrep = [bt1, bt1]

    hp_alias = {fc: [] for fc in range(FC)}
    for fc in range(16):
        hp_alias[fc].append(bwq)
    for fc in range(16, 20):
        hp_alias[fc].extend(bxn)
    for fc in range(20, 24):
        hp_alias[fc].extend(bao)
    for fc in range(24, 32):
        hp_alias[fc].extend([bmask, bp[0], bp[1], bsqb[0], bsqb[1], bonesb,
                             bwkv, bidb, bkvk])

    wq_r = wq.rearrange("(k p) d -> p k d", p=128)
    P.op("ld0", lambda e: e.dma_start(
        out=AR[:, WQ0:WQ0 + 8192].rearrange("p (k d) -> p k d", k=8), in_=wq_r),
        writes=[bwq])
    wkv_r = wkv.rearrange("(k p) d -> p k d", p=128)
    P.op("ld0", lambda e: e.dma_start(
        out=AR[:, WKV0:WKV0 + 512].rearrange("p (k d) -> p k d", k=8), in_=wkv_r),
        writes=[bwkv])
    wgt_r = wgt.rearrange("(k p) d -> p k d", p=128)
    P.op("ld0", lambda e: e.dma_start(
        out=wgts[:].rearrange("p (k d) -> p k d", k=8), in_=wgt_r),
        writes=[bwgt])
    P.op("ld0", lambda e: e.dma_start(out=kvf[0:64, 0:1], in_=nullkv), writes=[bkvf])

    P.op("spi", lambda e: e.dma_start(out=b1sb, in_=b1t), writes=[bb1])
    P.op("spi", lambda e: e.dma_start(out=b2sb, in_=b2t), writes=[bb2])
    P.op("spi", lambda e: e.dma_start(out=bgt_col, in_=bgt), writes=[bbgt])
    P.op("spi", lambda e: e.dma_start(out=AR[:, MK0:MK0 + 1280], in_=masks),
         writes=[bmask])
    xTo_r = xTo.rearrange("(c p) t -> p c t", p=128)
    for c in range(DC):
        P.op("spi", lambda e, _c=c: e.dma_start(out=xs[_c][:], in_=xTo_r[:, _c, :]),
             writes=[bxs[c]])

    P.op("gp", lambda e: e.memset(kvf[:, 1 + N:KVN], 0.0), writes=[bkvf])
    kvk3 = AR[:, KVK0:KVK0 + KC * 65].rearrange("p (c w) -> p c w", w=65)
    P.op("gp", lambda e: e.memset(kvk3[:, :, 64:65], 1.0), writes=[bkvk])
    P.op("gp", lambda e: e.memset(AR[:, KVK0 + KC * 65:KVK0 + KC * 65 + 63], 0.0),
         writes=[bkvk])
    def kvkp_v(kc):
        return AR[:, KVK0 + kc * 65: KVK0 + kc * 65 + 128]
    P.op("gp", lambda e: e.iota(fsc[:, 1312:1440], pattern=[[-1, 128]], base=0,
                                channel_multiplier=1,
                                allow_small_or_imprecise_dtypes=True),
         writes=[bidf])
    P.op("dve", lambda e: e.tensor_scalar(identf, fsc[:, 1312:1440], 0.0, None,
                                          AluOpType.is_equal),
         reads=[bidf], writes=[bidf])
    P.op("dve", lambda e: e.tensor_copy(identb, identf), reads=[bidf],
         writes=[bidb])
    P.op("gp", lambda e: e.memset(ones_f, 1.0), writes=[bonesf])
    P.op("gp", lambda e: e.memset(onesblk, 1.0), writes=[bonesf])
    P.op("gp", lambda e: e.memset(eps_col, LNEPS), writes=[bonesf])
    P.op("gp", lambda e: e.memset(ones_b, 1.0), writes=[bonesb])
    P.op("gp", lambda e: e.memset(row_kn2, 0.0), writes=[bkn2])

    LNS = [dict(m=row_m, ms=row_ms, var=row_var, tmp=row_tmp,
                rstd=row_rstd, mr=row_mr, sS=4, sSS=5,
                t1=lambda n: t1_v(n)),
           dict(m=fsc[0:1, 1536:2048], ms=fsc[32:33, 1536:2048],
                var=fsc[64:65, 1536:2048], tmp=fsc[96:97, 1536:2048],
                rstd=fsc[0:1, 2048:2560], mr=fsc[32:33, 2048:2560],
                sS=0, sSS=1,
                t1=lambda n: t1_v(n))]
    brow2 = Buf("rows2")
    LNB = [dict(row=brow, t1=bt1), dict(row=brow2, t1=bt1)]

    def ln_tile(src_v, src_bufs, out_v, out_bufs, n, par, sset=0):
        S = LNS[sset]
        rbuf, tbuf = LNB[sset]["row"], LNB[sset]["t1"]
        psS, bS = pb[S["sS"]], bpb[S["sS"]]
        psSS, bSS = pb[S["sSS"]], bpb[S["sSS"]]
        for c in range(DC):
            P.op("pe", lambda e, _c=c: e.matmul(
                psS[0:1, 0:n], ones_f, src_v(_c, 0, n),
                start=(_c == 0), stop=(_c == DC - 1)),
                reads=[bonesf, src_bufs[c]], writes=[bS])
        for c in range(DC):
            P.op("dve", lambda e, _c=c: e.tensor_tensor(
                sqb_v(par, n), src_v(_c, 0, n), src_v(_c, 0, n), AluOpType.mult),
                reads=[src_bufs[c]], writes=[bsqb[par]])
            P.op("pe", lambda e, _c=c: e.matmul(
                psSS[0:1, 0:n], ones_b, sqb_v(par, n),
                start=(_c == 0), stop=(_c == DC - 1)),
                reads=[bonesb, bsqb[par]], writes=[bSS])
        P.op("act", lambda e: e.activation(S["m"][:, 0:n], psS[0:1, 0:n], AF.Copy,
                                           scale=1.0 / D),
             reads=[bS], writes=[rbuf])
        P.op("act", lambda e: e.activation(S["ms"][:, 0:n], psS[0:1, 0:n], AF.Square,
                                           scale=1.0 / D),
             reads=[bS], writes=[rbuf])
        P.op("dve", lambda e: e.scalar_tensor_tensor(
            S["var"][:, 0:n], psSS[0:1, 0:n], 1.0 / D, S["ms"][:, 0:n],
            AluOpType.mult, AluOpType.subtract),
             reads=[bSS, rbuf], writes=[rbuf])
        P.op("act", lambda e: e.activation(S["tmp"][:, 0:n], S["var"][:, 0:n], AF.Ln,
                                           bias=eps_col[0:1, :]),
             reads=[rbuf, bonesf], writes=[rbuf])
        P.op("act", lambda e: e.activation(S["rstd"][:, 0:n], S["tmp"][:, 0:n], AF.Exp,
                                           scale=-0.5),
             reads=[rbuf], writes=[rbuf])
        P.op("dve", lambda e: e.tensor_tensor(S["mr"][:, 0:n], S["m"][:, 0:n],
                                              S["rstd"][:, 0:n], AluOpType.mult),
             reads=[rbuf], writes=[rbuf])
        P.op("pe", lambda e: e.matmul(pb[3][:, 0:n], onesblk[0:1, :],
                                      S["rstd"][:, 0:n], start=True, stop=True),
             reads=[rbuf, bonesf], writes=[bpb[3]])
        P.op("pe", lambda e: e.matmul(pb[2][:, 0:n], onesblk[32:33, :],
                                      S["mr"][:, 0:n], start=True, stop=True),
             reads=[rbuf, bonesf], writes=[bpb[2]])
        for c in range(DC):
            P.op("dve", lambda e, _c=c: e.tensor_tensor(
                S["t1"](n), src_v(_c, 0, n), pb[3][:, 0:n], AluOpType.mult),
                reads=[src_bufs[c], bpb[3]], writes=[tbuf])
            P.op("dve", lambda e, _c=c: e.tensor_tensor(
                out_v(_c, 0, n), S["t1"](n), pb[2][:, 0:n], AluOpType.subtract),
                reads=[tbuf, bpb[2]], writes=[out_bufs[c]])

    xT_r = xT.rearrange("(c p) t -> p c t", p=128)
    xnk_bufs = [bxn[c // 2] for c in range(DC)]
    bosc2 = Buf("osc2")

    def emit_kv_step(blk):
        P.op("osca" if blk % 2 == 0 else "oscb",
             lambda e, _b=blk: e.dma_start(
                 out=fsc[:, 3072:5120].rearrange("p (c t) -> p c t", c=8),
                 in_=xT_r[:, :, _b * 256:(_b + 1) * 256]),
             writes=[bosc2])
        ln_tile(lambda c, c0, n: osc_v(c, c0, n), [bosc2] * DC,
                lambda c, c0, n: xnk_v(c, n), xnk_bufs, 256, blk % 2,
                sset=blk % 2)
        for c in range(DC):
            P.op("pe", lambda e, _c=c: e.matmul(
                pb[6][0:64, 0:256], wkv_v(_c), xnk_v(_c),
                start=(_c == 0), stop=(_c == DC - 1)),
                reads=[bwkv, bxn[0], bxn[1], bxn[2], bxn[3]], writes=[bpb[6]])
        col = 1 + blk * 256
        P.op("act", lambda e, _col=col: e.activation(
            kvf[0:64, _col:_col + 256], pb[6][0:64, 0:256], AF.Copy),
            reads=[bpb[6]], writes=[bkvf])
        P.op("act", lambda e: e.activation(t1p_v(0, 64, 256), pb[6][0:64, 0:256],
                                           AF.Square),
             reads=[bpb[6]], writes=[bt1])
        P.op("pe", lambda e: e.matmul(pb[6][64:65, 0:256], ones_f[0:64, :],
                                      t1p_v(0, 64, 256), start=True, stop=True),
             reads=[bonesf, bt1], writes=[bpb[6]])
        P.op("dve", lambda e: e.reduce_max(row_sc1, pb[6][64:65, 0:256],
                                           axis=AX.X),
             reads=[bpb[6]], writes=[brow])
        P.op("dve", lambda e: e.tensor_tensor(row_kn2, row_kn2, row_sc1,
                                              AluOpType.max),
             reads=[brow], writes=[bkn2])
        P.op("spm", lambda e, _col=col: e.dma_start(
            out=kvf[64:128, _col:_col + 256], in_=kvf[0:64, _col:_col + 256]),
            reads=[bkvf], writes=[bkvf])
        chunks = [2 * blk, 2 * blk + 1] + ([32] if blk == 15 else [])
        for k in chunks:
            par = k % 4
            P.op("pe", lambda e, _k=k, _p=par: e.transpose(
                pbT[0:128, _p * 256:_p * 256 + 64],
                kvf[0:64, _k * 128:(_k + 1) * 128], identb[0:64, 0:64]),
                reads=[bkvf, bidb], writes=[bpbT])
            P.op("act", lambda e, _k=k, _p=par: e.activation(
                kvk3[:, _k, 0:64], pbT[0:128, _p * 256:_p * 256 + 64], AF.Copy),
                reads=[bpbT], writes=[bkvk])

    P.op("ld0", lambda e: e.dma_start(out=kvf[64:128, 0:1], in_=nullkv),
         writes=[bkvf])
    P.op("act", lambda e: e.activation(t1p_v(0, 64, 1), kvf[0:64, 0:1], AF.Square),
         reads=[bkvf], writes=[bt1])
    P.op("pe", lambda e: e.matmul(pb[6][64:65, 0:1], ones_f[0:64, :],
                                  t1p_v(0, 64, 1), start=True, stop=True),
         reads=[bonesf, bt1], writes=[bpb[6]])
    P.op("dve", lambda e: e.reduce_max(row_sc1, pb[6][64:65, 0:1], axis=AX.X),
         reads=[bpb[6]], writes=[brow])
    P.op("dve", lambda e: e.tensor_tensor(row_kn2, row_kn2, row_sc1, AluOpType.max),
         reads=[brow], writes=[bkn2])

    for blk in range(5):
        emit_kv_step(blk)

    for t in range(4):
        c0 = t * 512
        ln_tile(lambda c, cc, n, _c0=c0: xs[c][:, _c0 + cc:_c0 + cc + n], bxs,
                lambda c, cc, n: xnq_v(c, cc, n), bxn, 512, t % 2)
        for qc in range(DC):
            ppar = qc % 2
            for k in range(DC):
                P.op("pe", lambda e, _qc=qc, _k=k, _pp=ppar: e.matmul(
                    pb[_pp][:, 0:512], wq_v(_k, _qc * 128, 128), xnq_v(_k, 0, 512),
                    start=(_k == 0), stop=(_k == DC - 1)),
                    reads=[bwq, bxn[k]], writes=[bpb[ppar]])
            P.op("act", lambda e, _qc=qc, _pp=ppar, _c0=c0: e.activation(
                colA_v(_qc, _c0, 512), pb[_pp][:, 0:512], AF.Copy, scale=QSCALE),
                reads=[bpb[ppar]], writes=[bcolA[qc]])

    if "qb" in dbg_aps:
        P.op("st", lambda e: e.dma_start(out=dbg_aps["qb"], in_=AR[:, 0:16384]),
             reads=list(bcolA))

    P.op("ld0", lambda e: e.dma_start(out=AR[0:64, WQ0:WQ0 + 16384], in_=woutH),
         reads=[bwq], writes=[bwq])

    def emit_negc(s):
        if s >= NSLOT:
            return
        scol = s * SL
        par = s % 2
        for qc in range(DC):
            P.op("dve", lambda e, _qc=qc, _sc=scol: e.tensor_tensor(
                sqb_v(_qc % 2, SL), colA_v(_qc, _sc, SL), colA_v(_qc, _sc, SL),
                AluOpType.mult),
                reads=[bcolA[qc]], writes=[bsqb[qc % 2]])
            P.op("pe", lambda e, _qc=qc: e.matmul(
                pb[6][0:1, 0:SL], ones_b, sqb_v(_qc % 2, SL),
                start=(_qc == 0), stop=(_qc == DC - 1)),
                reads=[bonesb, bsqb[qc % 2]], writes=[bpb[6]])
        P.op("dve", lambda e: e.reduce_max(row_sc1, pb[6][0:1, 0:SL], axis=AX.X),
             reads=[bpb[6]], writes=[brow])
        P.op("dve", lambda e: e.tensor_tensor(row_sc2, row_sc1, row_kn2,
                                              AluOpType.mult),
             reads=[brow], writes=[brow])
        P.op("act", lambda e: e.activation(row_sc1, row_sc2, AF.Ln),
             reads=[brow], writes=[brow])
        P.op("act", lambda e: e.activation(row_sc2, row_sc1, AF.Exp, scale=0.5),
             reads=[brow], writes=[brow])
        P.op("act", lambda e: e.mul(row_sc1, row_sc2, -1.0),
             reads=[brow], writes=[brow])
        P.op("pe", lambda e: e.matmul(pb[6][0:128, 2:3], onesblk[64:65, :],
                                      row_sc1, start=True, stop=True),
             reads=[brow, bonesf], writes=[bpb[6]])
        P.op("act", lambda e, _p=par: e.activation(negc_cols[_p],
                                                   pb[6][0:128, 2:3], AF.Copy),
             reads=[bpb[6]], writes=[bnegc[par]])

    if stages >= 3:
        emit_negc(0)
    for s in range(ATTN_NSLOT if stages >= 3 else 0):
        if s >= 1:
            for blk in (2 * s + 3, 2 * s + 4):
                if blk <= 15:
                    emit_kv_step(blk)
        scol = s * SL
        Cs = 4 * s + 5
        KP = (Cs + 1) // 2
        pend = []

        def flush_pend():
            for f_ in pend:
                f_()
            pend.clear()

        for hp in range(8):
            qc = hp
            psOs = (pb[4], pb[5])
            bOs = (bpb[4], bpb[5])

            def emit_sims(kp, _qc=qc):
                kcs = [2 * kp] + ([2 * kp + 1] if 2 * kp + 1 < Cs else [])
                for hf in range(2):
                    sp = 2 * hf + (kp % 2)
                    for u, kc in enumerate(kcs):
                        P.op("pe", lambda e, _kc=kc, _sp=sp, _q=_qc, _hf=hf,
                             _u=u, _sc=scol: e.matmul(
                                 pb[_sp][:, _u * 256:_u * 256 + 256],
                                 kvf[64 * _hf:64 * (_hf + 1),
                                     _kc * 128:(_kc + 1) * 128],
                                 AR[64 * _hf:64 * (_hf + 1),
                                    _q * 2048 + _sc:_q * 2048 + _sc + SL],
                                 start=True, stop=True),
                             reads=[bkvf, bcolA[_qc]], writes=[bpb[sp]])

            emit_sims(0)
            flush_pend()
            for kp in range(KP):
                if kp + 1 < KP:
                    emit_sims(kp + 1)
                kcs = [2 * kp] + ([2 * kp + 1] if 2 * kp + 1 < Cs else [])
                w = 256 * len(kcs)
                pp = kp % 2
                j = kp - 2 * s
                for hf in range(2):
                    sp = 2 * hf + pp
                    P.op("act", lambda e, _sp=sp, _pp=pp, _hf=hf, _w=w,
                         _np=s % 2: e.activation(
                             p2_v(_hf, _pp)[:, 0:_w], pb[_sp][:, 0:_w],
                             AF.Exp, bias=negc_cols[_np]),
                         reads=[bpb[sp], bnegc[s % 2]], writes=[bp[2 * hf + pp]])
                    if 0 <= j <= 2:
                        P.op("dve", lambda e, _hf=hf, _pp=pp, _j=j, _w=w:
                             e.tensor_tensor(
                                 p2_v(_hf, _pp)[:, 0:_w],
                                 p2_v(_hf, _pp)[:, 0:_w],
                                 maskp_v(_j, _w), AluOpType.mult),
                             reads=[bmask], writes=[bp[2 * hf + pp]])
                    for u, kc in enumerate(kcs):
                        P.op("pe", lambda e, _kc=kc, _pp=pp, _ps=psOs[hf],
                             _C=Cs, _hf=hf, _u=u: e.matmul(
                                 _ps[0:128, 0:SL], kvkp_v(_kc),
                                 p2_v(_hf, _pp)[:, _u * 256:_u * 256 + 256],
                                 start=(_kc == 0), stop=(_kc == _C - 1)),
                             reads=[bkvk, bp[2 * hf + pp]], writes=[bOs[hf]])

            def emit_evac(_hp=hp, _psOs=psOs, _bOs=bOs):
                for hf in range(2):
                    P.op("dve", lambda e, _ps=_psOs[hf], _lp=hf: e.reciprocal(
                        row_linv[_lp], _ps[64:65, 0:SL]),
                        reads=[_bOs[hf]], writes=[blinv[hf]])
                    P.op("pe", lambda e, _lp=hf: e.matmul(
                        pb[6][0:64, _lp * 256:_lp * 256 + 256],
                        onesblk[64:65, 0:64], row_linv[_lp],
                        start=True, stop=True),
                        reads=[blinv[hf], bonesf], writes=[bpb[6]])
                    P.op("act", lambda e, _lp=hf: e.activation(
                        lrep_v(_lp), pb[6][0:64, _lp * 256:_lp * 256 + 256],
                        AF.Copy),
                        reads=[bpb[6]], writes=[blrep[hf]])
                    P.op("dve", lambda e, _ps=_psOs[hf], _lp=hf,
                         _h=2 * _hp + hf: e.tensor_tensor(
                            ao_v(_h), _ps[0:64, 0:SL], lrep_v(_lp),
                            AluOpType.mult),
                        reads=[_bOs[hf], blrep[hf]], writes=[bao[2 * _hp + hf]])

            pend.append(emit_evac)
        emit_negc(s + 1)
        flush_pend()
        for dd in range(DC):
            ppar = 4 + dd % 2
            for h in range(H):
                P.op("pe", lambda e, _dd=dd, _h=h, _pp=ppar: e.matmul(
                    pb[_pp][:, 0:SL], woutH_v(_h, _dd * 128, 128), ao_v(_h),
                    start=(_h == 0), stop=(_h == H - 1)),
                    reads=[bwq, bao[h]], writes=[bpb[ppar]])
            P.op("dve", lambda e, _dd=dd, _pp=ppar, _sc=scol: e.tensor_tensor(
                xs[_dd][:, _sc:_sc + SL], xs[_dd][:, _sc:_sc + SL],
                pb[_pp][:, 0:SL], AluOpType.add),
                reads=[bpb[ppar]], writes=[bxs[dd]])

    if "xattn" in dbg_aps:
        for c in range(DC):
            P.op("st", lambda e, _c=c: e.dma_start(
                out=dbg_aps["xattn"][_c * 128:(_c + 1) * 128, :], in_=xs[_c][:]),
                reads=[bxs[c]])
    if "ao" in dbg_aps:
        P.op("st", lambda e: e.dma_start(out=dbg_aps["ao"],
                                         in_=AR[0:64, AO0:AO0 + 4096]),
             reads=list(bao))
    if "negc" in dbg_aps:
        P.op("st", lambda e: e.dma_start(out=dbg_aps["negc"], in_=negc_cols[0]),
             reads=[bnegc[0]])


    if "lrow" in dbg_aps:
        P.op("st", lambda e: e.dma_start(out=dbg_aps["lrow"],
                                         in_=fsc[64:65, 0:512]),
             reads=[brow] + blinv)

    for t in range(4 if stages >= 4 else 0):
        c0 = t * 512
        ln_tile(lambda c, cc, n, _c0=c0: xs[c][:, _c0 + cc:_c0 + cc + n], bxs,
                lambda c, cc, n, _c0=c0: colA_v(c, _c0 + cc, n), bcolA, 512, t % 2)

    def emit_gate(t):
        c0 = t * 512
        for k in range(DC):
            P.op("pe", lambda e, _k=k, _c0=c0: e.matmul(
                pb[6][0:9, 0:512], wgt_v(_k), colA_v(_k, _c0, 512),
                start=(_k == 0), stop=(_k == DC - 1)),
                reads=[bwgt, bcolA[k]], writes=[bpb[6]])
        P.op("dve", lambda e: e.tensor_scalar(gsb, pb[6][0:9, 0:512], bgt_col,
                                              None, AluOpType.add),
             reads=[bpb[6], bbgt], writes=[bgsb, brow2])
        for u in range(4):
            u0 = u * 128
            P.op("pe", lambda e, _u0=u0: e.transpose(
                pb[4][:, 0:9], gsb[:, _u0:_u0 + 128], identf[0:9, 0:9]),
                reads=[bgsb, bidf], writes=[bpb[4]])
            P.op("dve", lambda e: e.reduce_max(g_negmax, pb[4][:, 0:8], axis=AX.X,
                                               negate=True),
                 reads=[bpb[4]], writes=[bgate])
            P.op("act", lambda e: e.activation(g_pg, pb[4][:, 0:8], AF.Exp,
                                               bias=g_negmax),
                 reads=[bpb[4], bgate], writes=[bgate])
            P.op("dve", lambda e: e.tensor_reduce(g_zi, g_pg, AX.X, AluOpType.add),
                 reads=[bgate], writes=[bgate])
            P.op("dve", lambda e: e.reciprocal(g_zi, g_zi),
                 reads=[bgate], writes=[bgate])
            P.op("dve", lambda e: e.tensor_scalar(g_pg, g_pg, g_zi, None,
                                                  AluOpType.mult),
                 reads=[bgate], writes=[bgate])
            P.op("act", lambda e: e.activation(g_thr, pb[4][:, 8:9], AF.Tanh,
                                               scale=0.5),
                 reads=[bpb[4]], writes=[bgate])
            P.op("dve", lambda e: e.tensor_scalar(g_thr, g_thr, 0.0625, 0.0625,
                                                  AluOpType.mult, AluOpType.add),
                 reads=[bgate], writes=[bgate])
            P.op("dve", lambda e: e.tensor_scalar(g_w, g_pg, g_thr, 0.0,
                                                  AluOpType.subtract,
                                                  AluOpType.max),
                 reads=[bgate], writes=[bgate])
            P.op("dve", lambda e: e.tensor_reduce(g_s, g_w, AX.X, AluOpType.add),
                 reads=[bgate], writes=[bgate])
            P.op("dve", lambda e: e.tensor_scalar(g_eq, g_s, 0.0, None,
                                                  AluOpType.is_equal),
                 reads=[bgate], writes=[bgate])
            P.op("dve", lambda e: e.tensor_tensor(g_s, g_s, g_eq, AluOpType.add),
                 reads=[bgate], writes=[bgate])
            P.op("dve", lambda e: e.reciprocal(g_s, g_s),
                 reads=[bgate], writes=[bgate])
            P.op("dve", lambda e: e.tensor_scalar(g_w, g_w, g_s, None,
                                                  AluOpType.mult),
                 reads=[bgate], writes=[bgate])
            P.op("pe", lambda e: e.transpose(pb[5][0:8, 0:128], g_w, identf),
                 reads=[bgate, bidf], writes=[bpb[5]])
            P.op("act", lambda e, _c0=c0, _u0=u0: e.activation(
                wfm[:, _c0 + _u0:_c0 + _u0 + 128], pb[5][0:8, 0:128], AF.Copy),
                reads=[bpb[5]], writes=[bwfm, bosc2])

    if stages >= 4:
        emit_gate(0)
        emit_gate(1)

    for e_ in range(E if stages >= 5 else 0):
        for ht in range(2):
            t0 = ht * 1024
            for fp2 in range(FC // 2):
                wpar = fp2 % 2
                P.op("wb1a" if wpar == 0 else "wb1b",
                     lambda e, _e=e_, _fp=fp2, _wp=wpar: e.dma_start(
                         out=wb1[_wp][:].rearrange("p (f c) -> p f c", f=2),
                         in_=w1t[_e, 2 * _fp:2 * _fp + 2].rearrange(
                             "f p c -> p f c")),
                     writes=[bwb1[wpar]])
                for fo in range(2):
                    f = 2 * fp2 + fo
                    for tt in range(2):
                        hpar = [0, 1, 6][(2 * f + tt) % 3]
                        for k in range(DC):
                            P.op("pe", lambda e, _fo=fo, _tt=tt, _k=k, _wp=wpar,
                                 _hp=hpar, _t0=t0: e.matmul(
                                     pb[_hp][:, 0:512],
                                     wb1[_wp][:, _fo * 1024 + _k * 128:
                                              _fo * 1024 + (_k + 1) * 128],
                                     colA_v(_k, _t0 + _tt * 512, 512),
                                     start=(_k == 0), stop=(_k == DC - 1)),
                                 reads=[bwb1[wpar], bcolA[k]], writes=[bpb[hpar]])
                        P.op("act", lambda e, _e=e_, _f=f, _tt=tt, _hp=hpar:
                             e.activation(
                                 hp_v(_f, _tt * 512, 512), pb[_hp][:, 0:512],
                                 AF.Gelu_apprx_tanh,
                                 bias=b1sb[:, _e * FC + _f:_e * FC + _f + 1]),
                             reads=[bpb[hpar], bb1] + (hp_alias[f]
                                                       if _ffn_first(e_, ht)
                                                       else []),
                             writes=[bhp[f]])
            if e_ == 0 and ht == 1:
                emit_gate(2)
                emit_gate(3)
            for tt in range(2):
                wr = fsc[64:65, 0:512] if tt == 0 else fsc[64:65, 512:1024]
                P.op("spm", lambda e, _e=e_, _t0=t0, _tt=tt, _wr=wr: e.dma_start(
                    out=_wr,
                    in_=wfm[_e:_e + 1,
                            _t0 + _tt * 512:_t0 + _tt * 512 + 512]),
                    reads=[bwfm], writes=[brow])
                P.op("pe", lambda e, _tt=tt, _wr=wr: e.matmul(
                    pb[4 + _tt][:, 0:512], onesblk[64:65, :], _wr,
                    start=True, stop=True),
                    reads=[brow, bonesf], writes=[bpb[4 + tt]])
                P.op("act", lambda e, _tt=tt: e.activation(
                    rep_v(_tt), pb[4 + _tt][:, 0:512], AF.Copy),
                    reads=[bpb[4 + tt]], writes=[brep[tt], brow2])
            for dd in range(DC):
                for fh in range(2):
                    wpar = fh
                    fam = "wb2a" if wpar == 0 else "wb2b"
                    P.op(fam, lambda e, _e=e_, _dd=dd, _fh=fh, _wp=wpar:
                         e.dma_start(
                             out=wb2[_wp][:],
                             in_=w2t[_e, _dd][:, _fh * 2048:(_fh + 1) * 2048]),
                         writes=[bwb2[wpar]])
                    for tt in range(2):
                        epar = 2 + (2 * dd + tt) % 4
                        for fc2 in range(16):
                            f = fh * 16 + fc2
                            P.op("pe", lambda e, _f=f, _fc=fc2, _wp=wpar, _tt=tt,
                                 _ep=epar, _t0=t0: e.matmul(
                                     pb[_ep][:, 0:512],
                                     wb2[_wp][:, _fc * 128:(_fc + 1) * 128],
                                     hp_v(_f, _tt * 512, 512),
                                     start=(_f == 0), stop=(_f == FC - 1)),
                                 reads=[bwb2[wpar], bhp[f]], writes=[bpb[epar]])
                for tt in range(2):
                    epar = 2 + (2 * dd + tt) % 4
                    P.op("dve", lambda e, _e=e_, _dd=dd, _tt=tt, _ep=epar:
                         e.scalar_tensor_tensor(
                             t1_v(512), pb[_ep][:, 0:512],
                             b2sb[:, _e * DC + _dd:_e * DC + _dd + 1],
                             rep_v(_tt), AluOpType.add, AluOpType.mult),
                         reads=[bpb[epar], bb2, brep[tt]], writes=[bt1])
                    P.op("dve", lambda e, _dd=dd, _tt=tt, _t0=t0: e.tensor_tensor(
                        xs[_dd][:, _t0 + _tt * 512:_t0 + _tt * 512 + 512],
                        xs[_dd][:, _t0 + _tt * 512:_t0 + _tt * 512 + 512],
                        t1_v(512), AluOpType.add),
                        reads=[bt1], writes=[bxs[dd]])

    for c in range(DC):
        P.op("st", lambda e, _c=c: e.dma_start(out=yT[_c * 128:(_c + 1) * 128, :],
                                               in_=xs[_c][:]),
             reads=[bxs[c]])
    if "kvf" in dbg_aps:
        P.op("st", lambda e: e.dma_start(out=dbg_aps["kvf"], in_=kvf[:]),
             reads=[bkvf])
    if "kvk" in dbg_aps:
        P.op("st", lambda e: e.dma_start(out=dbg_aps["kvk"],
                                         in_=AR[:, KVK0:KVK0 + KC * 65]),
             reads=[bkvk])
    if "xn2" in dbg_aps:
        P.op("st", lambda e: e.dma_start(out=dbg_aps["xn2"], in_=AR[:, 0:16384]),
             reads=list(bcolA))
    if "wfm" in dbg_aps:
        P.op("st", lambda e: e.dma_start(out=dbg_aps["wfm"], in_=wfm),
             reads=[bwfm])

    P.final_wait("sp", "st", P.cnt["st"])

    if not emit:
        stack.close()
        return P
    sem_stack = contextlib.ExitStack()
    sems = {}
    for name in ALL_ENGS:
        sems[name] = sem_stack.enter_context(nc.semaphore("s_" + name))
    with nc.Block() as block:

        @block.tensor
        def _(t):
            for fn in P.ops["pe"]:
                fn(t, sems)

        @block.scalar
        def _(t):
            for fn in P.ops["act"]:
                fn(t, sems)

        @block.vector
        def _(t):
            for fn in P.ops["dve"]:
                fn(t, sems)

        @block.gpsimd
        def _(t):
            for fn in P.ops["gp"]:
                fn(t, sems)

        @block.sync
        def _(t):
            for fn in P.ops["sp"]:
                fn(t, sems)

    sem_stack.close()
    stack.close()
    return P


def _ffn_first(e_, ht):
    return e_ == 0 and ht == 0


def build2(nc, dbg=(), stages=6):
    tmp = bass.Bass("TRN2", target_bir_lowering=False)
    P1 = build(tmp, dbg, stages, emit=False)
    return build(nc, dbg, stages, needed=P1.needed)


def _host_layout(inputs):
    x = np.asarray(inputs["L0_attn_x"], dtype=np.float32)
    Wq = np.asarray(inputs["L0_attn_Wq"], dtype=np.float32)
    Wkv = np.asarray(inputs["L0_attn_Wkv"], dtype=np.float32)
    null_kv = np.asarray(inputs["L0_attn_null_kv"], dtype=np.float32)
    Wout = np.asarray(inputs["L0_attn_Wout"], dtype=np.float32)
    Wg = np.asarray(inputs["L0_ffn_Wg"], dtype=np.float32)
    bg = np.asarray(inputs["L0_ffn_bg"], dtype=np.float32)
    Wt = np.asarray(inputs["L0_ffn_Wt"], dtype=np.float32)
    bt = np.asarray(inputs["L0_ffn_bt"], dtype=np.float32)
    W1 = np.asarray(inputs["L0_ffn_W1"], dtype=np.float32)
    b1 = np.asarray(inputs["L0_ffn_b1"], dtype=np.float32)
    W2 = np.asarray(inputs["L0_ffn_W2"], dtype=np.float32)
    b2 = np.asarray(inputs["L0_ffn_b2"], dtype=np.float32)

    shared = {
        "wq": np.ascontiguousarray(Wq),
        "wkv": np.ascontiguousarray(Wkv),
        "nullkv": np.ascontiguousarray(null_kv.reshape(DH, 1)),
        "woutH": np.ascontiguousarray(
            Wout.reshape(H, DH, D).transpose(1, 0, 2).reshape(DH, H * D)),
        "wgt": np.ascontiguousarray(np.concatenate([Wg, Wt], axis=1)),
        "bgt": np.ascontiguousarray(np.concatenate([bg, bt]).reshape(9, 1)),
        "w1t": np.ascontiguousarray(
            W1.reshape(E, 8, 128, FC, 128).transpose(0, 3, 2, 1, 4)
            .reshape(E, FC, 128, 1024)),
        "b1t": np.ascontiguousarray(
            b1.reshape(E, FC, 128).transpose(2, 0, 1).reshape(128, E * FC)),
        "w2t": np.ascontiguousarray(
            W2.reshape(E, FC, 128, DC, 128).transpose(0, 3, 2, 1, 4)
            .reshape(E, DC, 128, 4096)),
        "b2t": np.ascontiguousarray(
            b2.reshape(E, DC, 128).transpose(2, 0, 1).reshape(128, E * DC)),
    }
    jj = np.arange(128)[:, None]
    ii = np.arange(SL)[None, :]
    in_maps = []
    own_cols = []
    xTb_cache = {}
    for core in range(8):
        b = core % 4
        g = core // 4
        if b not in xTb_cache:
            xTb_cache[b] = np.ascontiguousarray(x[b].T)
        xTb = xTb_cache[b]
        cols = (np.arange(NSLOT)[:, None] * 512 + g * 256
                + np.arange(SL)[None, :]).reshape(-1)
        own_cols.append((b, cols))
        m = np.zeros((128, 1280), dtype=BF16NP)
        for r in range(5):
            mr = (jj <= ii + 1 + 256 * g - 128 * r).astype(BF16NP)
            m[:, r * SL:(r + 1) * SL] = mr
        im = dict(shared)
        im["xT"] = xTb
        im["xTo"] = np.ascontiguousarray(xTb[:, cols])
        im["masks"] = m
        in_maps.append(im)
    return in_maps, own_cols


def kernel(**inputs):
    in_maps, own_cols = _host_layout(inputs)
    nc = bass.Bass("TRN2", target_bir_lowering=False)
    build2(nc)
    res = run_bass_kernel_spmd(nc, in_maps, core_ids=list(range(8)))
    out = np.empty((B, N, D), dtype=np.float32)
    for core in range(8):
        b, cols = own_cols[core]
        out[b, cols, :] = res.results[core]["yT"].T
    return out


# version 13
# speedup vs baseline: 1.0085x; 1.0085x over previous
import contextlib
import numpy as np
import ml_dtypes
import concourse.bass as bass
import concourse.mybir as mybir
from concourse.alu_op_type import AluOpType
from concourse import library_config
from concourse.bass_utils import run_bass_kernel_spmd

F32 = mybir.dt.float32
BF16 = mybir.dt.bfloat16
AF = mybir.ActivationFunctionType
AX = mybir.AxisListType
BF16NP = ml_dtypes.bfloat16

B = 4
N = 4096
D = 1024
DC = 8
H = 16
DH = 64
E = 8
FF = 4096
FC = 32
TOWN = 2048
NSLOT = 8
SL = 256
KC = 33
KVN = KC * 128
LNEPS = 1e-5
QSCALE = DH ** -0.5
ATTN_SKEW = True
ATTN_NSLOT = NSLOT

DMA_FAMS = {"ld0": "gp", "wb1a": "gp", "wb1b": "gp", "wb2a": "gp",
            "wb2b": "gp", "spi": "sp", "osca": "sp", "oscb": "sp", "spm": "sp", "st": "sp"}
ALL_ENGS = ("pe", "act", "dve", "gp") + tuple(DMA_FAMS)


class Buf:
    __slots__ = ("name", "writer", "readers")

    def __init__(self, name):
        self.name = name
        self.writer = None
        self.readers = {}

    def __repr__(self):
        return f"Buf({self.name})"


class Prog:

    def __init__(self, needed=None):
        self.ops = {e: [] for e in ("pe", "act", "dve", "gp", "sp")}
        self.cnt = {e: 0 for e in ALL_ENGS}
        self.waited = {}
        self.drained = {e: 0 for e in ("pe", "act", "dve", "gp", "sp")}
        self.n_waits = 0
        self.n_drains = 0
        self.needed = needed if needed is not None else {e: set() for e in ALL_ENGS}
        self.pass2 = needed is not None
        self.semval = {e: 0 for e in ALL_ENGS}
        self.svmap = {e: {} for e in ALL_ENGS}
        self.n_incs = 0

    @staticmethod
    def _stream(eng):
        return DMA_FAMS.get(eng, eng)

    def op(self, eng, fn, reads=(), writes=(), tag=""):
        stream = self._stream(eng)
        is_dma = eng in DMA_FAMS
        waits = {}
        need_drain = False
        for b in tuple(reads) + tuple(writes):
            w = b.writer
            if w is not None:
                weng, wcnt = w
                if weng == eng and not is_dma:
                    if eng != "pe" and wcnt > self.drained[eng]:
                        need_drain = True
                elif weng != eng:
                    v = self.cnt[weng] if weng in DMA_FAMS else wcnt
                    if waits.get(weng, 0) < v:
                        waits[weng] = v
        for b in writes:
            for reng, rcnt in b.readers.items():
                if reng == eng and not is_dma:
                    if eng != "pe" and rcnt > self.drained[eng]:
                        need_drain = True
                elif reng != eng:
                    v = self.cnt[reng] if reng in DMA_FAMS else rcnt
                    if waits.get(reng, 0) < v:
                        waits[reng] = v
        final_waits = []
        for peng, val in waits.items():
            key = (stream, peng)
            if self.waited.get(key, 0) < val:
                self.waited[key] = val
                final_waits.append((peng, val))
                if not self.pass2:
                    self.needed[peng].add(val)
        inc = 16 if is_dma else 1
        self.cnt[eng] += inc
        my = self.cnt[eng]
        self.n_waits += len(final_waits)
        if need_drain:
            self.n_drains += 1
            self.drained[eng] = my - inc

        if self.pass2:
            do_inc = is_dma or my in self.needed[eng]
            if do_inc:
                self.semval[eng] += inc
                self.svmap[eng][my] = self.semval[eng]
                self.n_incs += 1
            tw = tuple((p, v if p in DMA_FAMS else self.svmap[p][v])
                       for p, v in final_waits)
        else:
            do_inc = True
            self.semval[eng] += inc
            self.svmap[eng][my] = self.semval[eng]
            tw = tuple(final_waits)

        def thunk(e, sems, _w=tw, _d=need_drain, _fn=fn,
                  _inc=inc, _eng=eng, _di=do_inc):
            for peng, val in _w:
                e.wait_ge(sems[peng], val)
            if _d:
                e.drain()
            inst = _fn(e)
            if _di:
                inst.then_inc(sems[_eng], _inc)

        self.ops[stream].append(thunk)
        for b in writes:
            b.writer = (eng, my)
            b.readers = {}
        for b in reads:
            b.readers[eng] = my
        return my

    def raw(self, stream, fn):
        self.ops[stream].append(lambda e, sems, _fn=fn: _fn(e))

    def final_wait(self, stream, eng, val):
        if not self.pass2:
            self.needed[eng].add(val)
        v = val if eng in DMA_FAMS else self.svmap[eng].get(val, val)
        self.ops[stream].append(
            lambda e, sems, _eng=eng, _v=v: e.wait_ge(sems[_eng], _v))


def build(nc, dbg=(), stages=6, needed=None, emit=True):
    P = Prog(needed)

    xT = nc.dram_tensor("xT", [D, N], F32, kind="ExternalInput").ap()
    xTo = nc.dram_tensor("xTo", [D, TOWN], F32, kind="ExternalInput").ap()
    wq = nc.dram_tensor("wq", [D, D], F32, kind="ExternalInput").ap()
    wkv = nc.dram_tensor("wkv", [D, DH], F32, kind="ExternalInput").ap()
    nullkv = nc.dram_tensor("nullkv", [DH, 1], F32, kind="ExternalInput").ap()
    woutH = nc.dram_tensor("woutH", [DH, H * D], F32, kind="ExternalInput").ap()
    wgt = nc.dram_tensor("wgt", [D, 9], F32, kind="ExternalInput").ap()
    bgt = nc.dram_tensor("bgt", [9, 1], F32, kind="ExternalInput").ap()
    w1t = nc.dram_tensor("w1t", [E, FC, 128, 1024], F32, kind="ExternalInput").ap()
    b1t = nc.dram_tensor("b1t", [128, E * FC], F32, kind="ExternalInput").ap()
    w2t = nc.dram_tensor("w2t", [E, DC, 128, 4096], F32, kind="ExternalInput").ap()
    b2t = nc.dram_tensor("b2t", [128, E * DC], F32, kind="ExternalInput").ap()
    masks = nc.dram_tensor("masks", [128, 1280], BF16, kind="ExternalInput").ap()
    yT = nc.dram_tensor("yT", [D, TOWN], F32, kind="ExternalOutput").ap()

    dbg_aps = {}
    for name, shape, dt in (("kvf", [128, KVN], BF16),
                            ("qb", [128, 16384], BF16),
                            ("xn2", [128, 16384], BF16),
                            ("wfm", [8, 2048], F32),
                            ("xattn", [D, TOWN], F32),
                            ("kvk", [128, KC * 65], BF16),
                            ("ao", [64, 4096], BF16),
                            ("negc", [128, 1], F32),
                            ("mk", [128, 1280], BF16),
                            ("pp", [128, 2 * SL], BF16),
                            ("lrow", [1, 2 * SL], F32)):
        if name in dbg:
            dbg_aps[name] = nc.dram_tensor("dbg_" + name, shape, dt,
                                           kind="ExternalOutput").ap()

    stack = contextlib.ExitStack()
    sbt = lambda name, shape, dt: stack.enter_context(nc.sbuf_tensor(name, shape, dt))
    pst = lambda name, shape, dt: stack.enter_context(nc.psum_tensor(name, shape, dt))

    xs = [sbt(f"xs{c}", [128, TOWN], F32) for c in range(DC)]
    AR = sbt("arena", [128, 49152], BF16)
    kvf = sbt("kvf", [128, KVN], BF16)
    wb1 = [sbt(f"wb1_{i}", [128, 2048], BF16) for i in range(2)]
    wb2 = [sbt(f"wb2_{i}", [128, 2048], BF16) for i in range(2)]
    fsc = sbt("fsc", [128, 5248], F32)
    wgts = sbt("wgts", [128, 72], BF16)

    def colA_v(c, c0, n):
        return AR[:, c * 2048 + c0: c * 2048 + c0 + n]

    WQ0 = 16384
    def wq_v(k, j0, n):
        return AR[:, WQ0 + k * 1024 + j0: WQ0 + k * 1024 + j0 + n]

    def woutH_v(h, j0, n):
        return AR[0:64, WQ0 + h * 1024 + j0: WQ0 + h * 1024 + j0 + n]

    XN0 = 32768
    def xnk_v(c, n=256, par=0):
        return AR[:, XN0 + par * 2048 + c * 256: XN0 + par * 2048 + c * 256 + n]

    def xnq_v(c, c0, n):
        return AR[:, XN0 + c * 512 + c0: XN0 + c * 512 + c0 + n]

    AO0 = 36864
    def ao_v(h):
        return AR[0:64, AO0 + h * SL: AO0 + (h + 1) * SL]

    MK0 = 40960
    def maskp_v(j, n=512):
        return AR[:, MK0 + j * 512: MK0 + j * 512 + n]

    PB0 = 42240
    def p2_v(hf, par):
        i = 2 * hf + par
        return AR[:, PB0 + i * 512: PB0 + (i + 1) * 512]

    SQ0 = 44544
    def sqb_v(par, n=512):
        return AR[:, SQ0 + par * 512: SQ0 + par * 512 + n]

    ones_b = AR[:, 45568:45569]
    WKV0 = 45584
    def wkv_v(k):
        return AR[:, WKV0 + k * 64: WKV0 + (k + 1) * 64]

    def wgt_v(k):
        return wgts[:, k * 9:(k + 1) * 9]

    IDB0 = 46176
    identb = AR[:, IDB0:IDB0 + 128]
    KVK0 = 46336
    kvk = AR[:, KVK0:KVK0 + KC * 65 + 63]

    def hp_v(fc, c0, n):
        return AR[:, 16384 + fc * 1024 + c0: 16384 + fc * 1024 + c0 + n]

    row_m = fsc[0:1, 0:512]
    row_ms = fsc[32:33, 0:512]
    row_var = fsc[64:65, 0:512]
    row_tmp = fsc[96:97, 0:512]
    row_rstd = fsc[0:1, 512:1024]
    row_mr = fsc[32:33, 512:1024]
    wrow = fsc[96:97, 512:1024]
    row_kn2 = fsc[64:65, 512:513]
    row_sc1 = fsc[64:65, 513:514]
    row_sc2 = fsc[64:65, 514:515]
    row_linv = [fsc[64:65, 0:256], fsc[64:65, 256:512]]
    ones_f = fsc[:, 1024:1025]
    negc_cols = [fsc[:, 1025:1026], fsc[:, 1033:1034]]
    eps_col = fsc[:, 1026:1027]
    g_negmax = fsc[:, 1027:1028]
    g_zi = fsc[:, 1028:1029]
    g_thr = fsc[:, 1029:1030]
    g_s = fsc[:, 1030:1031]
    g_eq = fsc[:, 1031:1032]
    bgt_col = fsc[0:9, 1032:1033]
    g_pg = fsc[:, 1036:1044]
    g_w = fsc[:, 1044:1052]
    b1sb = fsc[:, 1056:1312]
    identf = fsc[:, 1312:1440]
    b2sb = fsc[:, 1440:1504]
    REP = (1536, 2048)
    def rep_v(i, n=512):
        return fsc[:, REP[i]:REP[i] + n]
    gsb = fsc[0:9, 1536:2048]
    T10 = 2560
    def t1_v(n=512):
        return fsc[:, T10:T10 + n]
    def t1p_v(p0, p1, n):
        return fsc[p0:p1, T10:T10 + n]
    def lrep_v(par):
        return fsc[0:64, T10 + par * SL: T10 + (par + 1) * SL]
    wfm = fsc[0:8, 3072:5120]
    def osc_v(c, c0, n):
        return fsc[:, 3072 + c * 256 + c0: 3072 + c * 256 + c0 + n]
    onesblk = fsc[:, 5120:5248]

    pb = [pst(f"pb{i}", [128, 512], F32) for i in range(7)]
    pbT = pst("pbT", [128, 1024], BF16)

    bxs = [Buf(f"xs{c}") for c in range(DC)]
    bcolA = [Buf(f"colA{c}") for c in range(DC)]
    bkvf = Buf("kvf")
    bkvk = Buf("kvk")
    bwb1 = [Buf(f"wb1_{i}") for i in range(2)]
    bwb2 = [Buf(f"wb2_{i}") for i in range(2)]
    bwq = Buf("wq_wout")
    bxn = [Buf(f"xn{c}") for c in range(DC)]
    bao = [Buf(f"ao{h}") for h in range(H)]
    bmask = Buf("mask")
    bp = [Buf(f"p{i}") for i in range(4)]
    bsqb = [Buf(f"sqb{i}") for i in range(2)]
    bonesb = Buf("onesb")
    bonesf = Buf("onesf")
    bidf = Buf("identf")
    bidb = Buf("identb")
    bwkv = Buf("wkv")
    bwgt = Buf("wgt")
    bosc = Buf("osc")
    brow = Buf("rows")
    bnegc = [Buf("negc0"), Buf("negc1")]
    blinv = [Buf(f"linv{i}") for i in range(2)]
    brep = [Buf(f"rep{i}") for i in range(2)]
    bb1 = Buf("b1")
    bb2 = Buf("b2")
    bbgt = Buf("bgt")
    bgate = Buf("gatescratch")
    bgsb = brep[0]
    bwfm = Buf("wfm")
    bhp = [Buf(f"hp{fc}") for fc in range(FC)]
    bpb = [Buf(f"pb{i}") for i in range(7)]
    bpbT = Buf("pbT")
    bkn2 = brow
    bt1 = Buf("t1")
    blrep = [bt1, bt1]

    hp_alias = {fc: [] for fc in range(FC)}
    for fc in range(16):
        hp_alias[fc].append(bwq)
    for fc in range(16, 20):
        hp_alias[fc].extend(bxn)
    for fc in range(20, 24):
        hp_alias[fc].extend(bao)
    for fc in range(24, 32):
        hp_alias[fc].extend([bmask, bp[0], bp[1], bsqb[0], bsqb[1], bonesb,
                             bwkv, bidb, bkvk])

    wq_r = wq.rearrange("(k p) d -> p k d", p=128)
    P.op("ld0", lambda e: e.dma_start(
        out=AR[:, WQ0:WQ0 + 8192].rearrange("p (k d) -> p k d", k=8), in_=wq_r),
        writes=[bwq])
    wkv_r = wkv.rearrange("(k p) d -> p k d", p=128)
    P.op("ld0", lambda e: e.dma_start(
        out=AR[:, WKV0:WKV0 + 512].rearrange("p (k d) -> p k d", k=8), in_=wkv_r),
        writes=[bwkv])
    wgt_r = wgt.rearrange("(k p) d -> p k d", p=128)
    P.op("ld0", lambda e: e.dma_start(
        out=wgts[:].rearrange("p (k d) -> p k d", k=8), in_=wgt_r),
        writes=[bwgt])
    P.op("ld0", lambda e: e.dma_start(out=kvf[0:64, 0:1], in_=nullkv), writes=[bkvf])

    P.op("spi", lambda e: e.dma_start(out=b1sb, in_=b1t), writes=[bb1])
    P.op("spi", lambda e: e.dma_start(out=b2sb, in_=b2t), writes=[bb2])
    P.op("spi", lambda e: e.dma_start(out=bgt_col, in_=bgt), writes=[bbgt])
    P.op("spi", lambda e: e.dma_start(out=AR[:, MK0:MK0 + 1280], in_=masks),
         writes=[bmask])
    xTo_r = xTo.rearrange("(c p) t -> p c t", p=128)
    for c in range(DC):
        P.op("spi", lambda e, _c=c: e.dma_start(out=xs[_c][:], in_=xTo_r[:, _c, :]),
             writes=[bxs[c]])

    P.op("gp", lambda e: e.memset(kvf[:, 1 + N:KVN], 0.0), writes=[bkvf])
    kvk3 = AR[:, KVK0:KVK0 + KC * 65].rearrange("p (c w) -> p c w", w=65)
    P.op("gp", lambda e: e.memset(kvk3[:, :, 64:65], 1.0), writes=[bkvk])
    P.op("gp", lambda e: e.memset(AR[:, KVK0 + KC * 65:KVK0 + KC * 65 + 63], 0.0),
         writes=[bkvk])
    def kvkp_v(kc):
        return AR[:, KVK0 + kc * 65: KVK0 + kc * 65 + 128]
    P.op("gp", lambda e: e.iota(fsc[:, 1312:1440], pattern=[[-1, 128]], base=0,
                                channel_multiplier=1,
                                allow_small_or_imprecise_dtypes=True),
         writes=[bidf])
    P.op("dve", lambda e: e.tensor_scalar(identf, fsc[:, 1312:1440], 0.0, None,
                                          AluOpType.is_equal),
         reads=[bidf], writes=[bidf])
    P.op("dve", lambda e: e.tensor_copy(identb, identf), reads=[bidf],
         writes=[bidb])
    P.op("gp", lambda e: e.memset(ones_f, 1.0), writes=[bonesf])
    P.op("gp", lambda e: e.memset(onesblk, 1.0), writes=[bonesf])
    P.op("gp", lambda e: e.memset(eps_col, LNEPS), writes=[bonesf])
    P.op("gp", lambda e: e.memset(ones_b, 1.0), writes=[bonesb])
    P.op("gp", lambda e: e.memset(row_kn2, 0.0), writes=[bkn2])

    LNS = [dict(m=row_m, ms=row_ms, var=row_var, tmp=row_tmp,
                rstd=row_rstd, mr=row_mr, sS=4, sSS=5,
                t1=lambda n: t1_v(n)),
           dict(m=fsc[0:1, 1536:2048], ms=fsc[32:33, 1536:2048],
                var=fsc[64:65, 1536:2048], tmp=fsc[96:97, 1536:2048],
                rstd=fsc[0:1, 2048:2560], mr=fsc[32:33, 2048:2560],
                sS=0, sSS=1,
                t1=lambda n: t1_v(n))]
    brow2 = Buf("rows2")
    LNB = [dict(row=brow, t1=bt1), dict(row=brow2, t1=bt1)]

    def ln_tile(src_v, src_bufs, out_v, out_bufs, n, par, sset=0):
        S = LNS[sset]
        rbuf, tbuf = LNB[sset]["row"], LNB[sset]["t1"]
        psS, bS = pb[S["sS"]], bpb[S["sS"]]
        psSS, bSS = pb[S["sSS"]], bpb[S["sSS"]]
        for c in range(DC):
            P.op("pe", lambda e, _c=c: e.matmul(
                psS[0:1, 0:n], ones_f, src_v(_c, 0, n),
                start=(_c == 0), stop=(_c == DC - 1)),
                reads=[bonesf, src_bufs[c]], writes=[bS])
        for c in range(DC):
            P.op("dve", lambda e, _c=c: e.tensor_tensor(
                sqb_v(par, n), src_v(_c, 0, n), src_v(_c, 0, n), AluOpType.mult),
                reads=[src_bufs[c]], writes=[bsqb[par]])
            P.op("pe", lambda e, _c=c: e.matmul(
                psSS[0:1, 0:n], ones_b, sqb_v(par, n),
                start=(_c == 0), stop=(_c == DC - 1)),
                reads=[bonesb, bsqb[par]], writes=[bSS])
        P.op("act", lambda e: e.activation(S["m"][:, 0:n], psS[0:1, 0:n], AF.Copy,
                                           scale=1.0 / D),
             reads=[bS], writes=[rbuf])
        P.op("act", lambda e: e.activation(S["ms"][:, 0:n], psS[0:1, 0:n], AF.Square,
                                           scale=1.0 / D),
             reads=[bS], writes=[rbuf])
        P.op("dve", lambda e: e.scalar_tensor_tensor(
            S["var"][:, 0:n], psSS[0:1, 0:n], 1.0 / D, S["ms"][:, 0:n],
            AluOpType.mult, AluOpType.subtract),
             reads=[bSS, rbuf], writes=[rbuf])
        P.op("act", lambda e: e.activation(S["tmp"][:, 0:n], S["var"][:, 0:n], AF.Ln,
                                           bias=eps_col[0:1, :]),
             reads=[rbuf, bonesf], writes=[rbuf])
        P.op("act", lambda e: e.activation(S["rstd"][:, 0:n], S["tmp"][:, 0:n], AF.Exp,
                                           scale=-0.5),
             reads=[rbuf], writes=[rbuf])
        P.op("dve", lambda e: e.tensor_tensor(S["mr"][:, 0:n], S["m"][:, 0:n],
                                              S["rstd"][:, 0:n], AluOpType.mult),
             reads=[rbuf], writes=[rbuf])
        P.op("pe", lambda e: e.matmul(pb[3][:, 0:n], onesblk[0:1, :],
                                      S["rstd"][:, 0:n], start=True, stop=True),
             reads=[rbuf, bonesf], writes=[bpb[3]])
        P.op("pe", lambda e: e.matmul(pb[2][:, 0:n], onesblk[32:33, :],
                                      S["mr"][:, 0:n], start=True, stop=True),
             reads=[rbuf, bonesf], writes=[bpb[2]])
        for c in range(DC):
            P.op("dve", lambda e, _c=c: e.tensor_tensor(
                S["t1"](n), src_v(_c, 0, n), pb[3][:, 0:n], AluOpType.mult),
                reads=[src_bufs[c], bpb[3]], writes=[tbuf])
            P.op("dve", lambda e, _c=c: e.tensor_tensor(
                out_v(_c, 0, n), S["t1"](n), pb[2][:, 0:n], AluOpType.subtract),
                reads=[tbuf, bpb[2]], writes=[out_bufs[c]])

    xT_r = xT.rearrange("(c p) t -> p c t", p=128)
    xnk_bufs = [bxn[c // 2] for c in range(DC)]
    bosc2 = Buf("osc2")

    def emit_kv_ln(blk):
        par = blk % 2
        P.op("osca" if par == 0 else "oscb",
             lambda e, _b=blk: e.dma_start(
                 out=fsc[:, 3072:5120].rearrange("p (c t) -> p c t", c=8),
                 in_=xT_r[:, :, _b * 256:(_b + 1) * 256]),
             writes=[bosc2])
        ln_tile(lambda c, c0, n: osc_v(c, c0, n), [bosc2] * DC,
                lambda c, c0, n, _p=par: xnk_v(c, n, _p),
                [bxn[2 * par + c // 4] for c in range(DC)], 256, par,
                sset=par)

    def emit_kv_tail(blk):
        par = blk % 2
        for c in range(DC):
            P.op("pe", lambda e, _c=c, _p=par: e.matmul(
                pb[6][0:64, 0:256], wkv_v(_c), xnk_v(_c, 256, _p),
                start=(_c == 0), stop=(_c == DC - 1)),
                reads=[bwkv, bxn[2 * par], bxn[2 * par + 1]], writes=[bpb[6]])
        col = 1 + blk * 256
        P.op("act", lambda e, _col=col: e.activation(
            kvf[0:64, _col:_col + 256], pb[6][0:64, 0:256], AF.Copy),
            reads=[bpb[6]], writes=[bkvf])
        P.op("act", lambda e: e.activation(t1p_v(0, 64, 256), pb[6][0:64, 0:256],
                                           AF.Square),
             reads=[bpb[6]], writes=[bt1])
        P.op("pe", lambda e: e.matmul(pb[6][64:65, 0:256], ones_f[0:64, :],
                                      t1p_v(0, 64, 256), start=True, stop=True),
             reads=[bonesf, bt1], writes=[bpb[6]])
        P.op("dve", lambda e: e.reduce_max(row_sc1, pb[6][64:65, 0:256],
                                           axis=AX.X),
             reads=[bpb[6]], writes=[brow])
        P.op("dve", lambda e: e.tensor_tensor(row_kn2, row_kn2, row_sc1,
                                              AluOpType.max),
             reads=[brow], writes=[bkn2])
        P.op("spm", lambda e, _col=col: e.dma_start(
            out=kvf[64:128, _col:_col + 256], in_=kvf[0:64, _col:_col + 256]),
            reads=[bkvf], writes=[bkvf])
        chunks = [2 * blk, 2 * blk + 1] + ([32] if blk == 15 else [])
        for k in chunks:
            cpar = k % 4
            P.op("pe", lambda e, _k=k, _p=cpar: e.transpose(
                pbT[0:128, _p * 256:_p * 256 + 64],
                kvf[0:64, _k * 128:(_k + 1) * 128], identb[0:64, 0:64]),
                reads=[bkvf, bidb], writes=[bpbT])
            P.op("act", lambda e, _k=k, _p=cpar: e.activation(
                kvk3[:, _k, 0:64], pbT[0:128, _p * 256:_p * 256 + 64], AF.Copy),
                reads=[bpbT], writes=[bkvk])

    def emit_kv_step(blk):
        emit_kv_ln(blk)
        emit_kv_tail(blk)

    P.op("ld0", lambda e: e.dma_start(out=kvf[64:128, 0:1], in_=nullkv),
         writes=[bkvf])
    P.op("act", lambda e: e.activation(t1p_v(0, 64, 1), kvf[0:64, 0:1], AF.Square),
         reads=[bkvf], writes=[bt1])
    P.op("pe", lambda e: e.matmul(pb[6][64:65, 0:1], ones_f[0:64, :],
                                  t1p_v(0, 64, 1), start=True, stop=True),
         reads=[bonesf, bt1], writes=[bpb[6]])
    P.op("dve", lambda e: e.reduce_max(row_sc1, pb[6][64:65, 0:1], axis=AX.X),
         reads=[bpb[6]], writes=[brow])
    P.op("dve", lambda e: e.tensor_tensor(row_kn2, row_kn2, row_sc1, AluOpType.max),
         reads=[brow], writes=[bkn2])

    for blk in range(5):
        emit_kv_step(blk)

    for t in range(4):
        c0 = t * 512
        ln_tile(lambda c, cc, n, _c0=c0: xs[c][:, _c0 + cc:_c0 + cc + n], bxs,
                lambda c, cc, n: xnq_v(c, cc, n), bxn, 512, t % 2)
        for qc in range(DC):
            ppar = qc % 2
            for k in range(DC):
                P.op("pe", lambda e, _qc=qc, _k=k, _pp=ppar: e.matmul(
                    pb[_pp][:, 0:512], wq_v(_k, _qc * 128, 128), xnq_v(_k, 0, 512),
                    start=(_k == 0), stop=(_k == DC - 1)),
                    reads=[bwq, bxn[k]], writes=[bpb[ppar]])
            P.op("act", lambda e, _qc=qc, _pp=ppar, _c0=c0: e.activation(
                colA_v(_qc, _c0, 512), pb[_pp][:, 0:512], AF.Copy, scale=QSCALE),
                reads=[bpb[ppar]], writes=[bcolA[qc]])

    if "qb" in dbg_aps:
        P.op("st", lambda e: e.dma_start(out=dbg_aps["qb"], in_=AR[:, 0:16384]),
             reads=list(bcolA))

    P.op("ld0", lambda e: e.dma_start(out=AR[0:64, WQ0:WQ0 + 16384], in_=woutH),
         reads=[bwq], writes=[bwq])

    def emit_negc(s):
        if s >= NSLOT:
            return
        scol = s * SL
        par = s % 2
        for qc in range(DC):
            P.op("dve", lambda e, _qc=qc, _sc=scol: e.tensor_tensor(
                sqb_v(_qc % 2, SL), colA_v(_qc, _sc, SL), colA_v(_qc, _sc, SL),
                AluOpType.mult),
                reads=[bcolA[qc]], writes=[bsqb[qc % 2]])
            P.op("pe", lambda e, _qc=qc: e.matmul(
                pb[6][0:1, 0:SL], ones_b, sqb_v(_qc % 2, SL),
                start=(_qc == 0), stop=(_qc == DC - 1)),
                reads=[bonesb, bsqb[qc % 2]], writes=[bpb[6]])
        P.op("dve", lambda e: e.reduce_max(row_sc1, pb[6][0:1, 0:SL], axis=AX.X),
             reads=[bpb[6]], writes=[brow])
        P.op("dve", lambda e: e.tensor_tensor(row_sc2, row_sc1, row_kn2,
                                              AluOpType.mult),
             reads=[brow], writes=[brow])
        P.op("act", lambda e: e.activation(row_sc1, row_sc2, AF.Ln),
             reads=[brow], writes=[brow])
        P.op("act", lambda e: e.activation(row_sc2, row_sc1, AF.Exp, scale=0.5),
             reads=[brow], writes=[brow])
        P.op("act", lambda e: e.mul(row_sc1, row_sc2, -1.0),
             reads=[brow], writes=[brow])
        P.op("pe", lambda e: e.matmul(pb[6][0:128, 2:3], onesblk[64:65, :],
                                      row_sc1, start=True, stop=True),
             reads=[brow, bonesf], writes=[bpb[6]])
        P.op("act", lambda e, _p=par: e.activation(negc_cols[_p],
                                                   pb[6][0:128, 2:3], AF.Copy),
             reads=[bpb[6]], writes=[bnegc[par]])

    if stages >= 3:
        emit_negc(0)
    for s in range(ATTN_NSLOT if stages >= 3 else 0):
        if s >= 1 and 2 * s + 3 <= 15:
            emit_kv_ln(2 * s + 3)
        scol = s * SL
        Cs = 4 * s + 5
        KP = (Cs + 1) // 2
        pend = []

        def flush_pend():
            for f_ in pend:
                f_()
            pend.clear()

        for hp in range(8):
            qc = hp
            psOs = (pb[4], pb[5])
            bOs = (bpb[4], bpb[5])

            def emit_sims(kp, _qc=qc):
                kcs = [2 * kp] + ([2 * kp + 1] if 2 * kp + 1 < Cs else [])
                for hf in range(2):
                    sp = 2 * hf + (kp % 2)
                    for u, kc in enumerate(kcs):
                        P.op("pe", lambda e, _kc=kc, _sp=sp, _q=_qc, _hf=hf,
                             _u=u, _sc=scol: e.matmul(
                                 pb[_sp][:, _u * 256:_u * 256 + 256],
                                 kvf[64 * _hf:64 * (_hf + 1),
                                     _kc * 128:(_kc + 1) * 128],
                                 AR[64 * _hf:64 * (_hf + 1),
                                    _q * 2048 + _sc:_q * 2048 + _sc + SL],
                                 start=True, stop=True),
                             reads=[bkvf, bcolA[_qc]], writes=[bpb[sp]])

            emit_sims(0)
            flush_pend()
            for kp in range(KP):
                if kp + 1 < KP:
                    emit_sims(kp + 1)
                kcs = [2 * kp] + ([2 * kp + 1] if 2 * kp + 1 < Cs else [])
                w = 256 * len(kcs)
                pp = kp % 2
                j = kp - 2 * s
                for hf in range(2):
                    sp = 2 * hf + pp
                    P.op("act", lambda e, _sp=sp, _pp=pp, _hf=hf, _w=w,
                         _np=s % 2: e.activation(
                             p2_v(_hf, _pp)[:, 0:_w], pb[_sp][:, 0:_w],
                             AF.Exp, bias=negc_cols[_np]),
                         reads=[bpb[sp], bnegc[s % 2]], writes=[bp[2 * hf + pp]])
                    if 0 <= j <= 2:
                        P.op("dve", lambda e, _hf=hf, _pp=pp, _j=j, _w=w:
                             e.tensor_tensor(
                                 p2_v(_hf, _pp)[:, 0:_w],
                                 p2_v(_hf, _pp)[:, 0:_w],
                                 maskp_v(_j, _w), AluOpType.mult),
                             reads=[bmask], writes=[bp[2 * hf + pp]])
                    for u, kc in enumerate(kcs):
                        P.op("pe", lambda e, _kc=kc, _pp=pp, _ps=psOs[hf],
                             _C=Cs, _hf=hf, _u=u: e.matmul(
                                 _ps[0:128, 0:SL], kvkp_v(_kc),
                                 p2_v(_hf, _pp)[:, _u * 256:_u * 256 + 256],
                                 start=(_kc == 0), stop=(_kc == _C - 1)),
                             reads=[bkvk, bp[2 * hf + pp]], writes=[bOs[hf]])

            def emit_evac(_hp=hp, _psOs=psOs, _bOs=bOs):
                for hf in range(2):
                    P.op("dve", lambda e, _ps=_psOs[hf], _lp=hf: e.reciprocal(
                        row_linv[_lp], _ps[64:65, 0:SL]),
                        reads=[_bOs[hf]], writes=[blinv[hf]])
                    P.op("pe", lambda e, _lp=hf: e.matmul(
                        pb[6][0:64, _lp * 256:_lp * 256 + 256],
                        onesblk[64:65, 0:64], row_linv[_lp],
                        start=True, stop=True),
                        reads=[blinv[hf], bonesf], writes=[bpb[6]])
                    P.op("act", lambda e, _lp=hf: e.activation(
                        lrep_v(_lp), pb[6][0:64, _lp * 256:_lp * 256 + 256],
                        AF.Copy),
                        reads=[bpb[6]], writes=[blrep[hf]])
                    P.op("dve", lambda e, _ps=_psOs[hf], _lp=hf,
                         _h=2 * _hp + hf: e.tensor_tensor(
                            ao_v(_h), _ps[0:64, 0:SL], lrep_v(_lp),
                            AluOpType.mult),
                        reads=[_bOs[hf], blrep[hf]], writes=[bao[2 * _hp + hf]])

            pend.append(emit_evac)
            if s >= 1:
                if hp == 2 and 2 * s + 3 <= 15:
                    flush_pend()
                    emit_kv_tail(2 * s + 3)
                elif hp == 3 and 2 * s + 4 <= 15:
                    flush_pend()
                    emit_kv_ln(2 * s + 4)
                elif hp == 6 and 2 * s + 4 <= 15:
                    flush_pend()
                    emit_kv_tail(2 * s + 4)
        emit_negc(s + 1)
        flush_pend()
        for dd in range(DC):
            ppar = 4 + dd % 2
            for h in range(H):
                P.op("pe", lambda e, _dd=dd, _h=h, _pp=ppar: e.matmul(
                    pb[_pp][:, 0:SL], woutH_v(_h, _dd * 128, 128), ao_v(_h),
                    start=(_h == 0), stop=(_h == H - 1)),
                    reads=[bwq, bao[h]], writes=[bpb[ppar]])
            P.op("dve", lambda e, _dd=dd, _pp=ppar, _sc=scol: e.tensor_tensor(
                xs[_dd][:, _sc:_sc + SL], xs[_dd][:, _sc:_sc + SL],
                pb[_pp][:, 0:SL], AluOpType.add),
                reads=[bpb[ppar]], writes=[bxs[dd]])

    if "xattn" in dbg_aps:
        for c in range(DC):
            P.op("st", lambda e, _c=c: e.dma_start(
                out=dbg_aps["xattn"][_c * 128:(_c + 1) * 128, :], in_=xs[_c][:]),
                reads=[bxs[c]])
    if "ao" in dbg_aps:
        P.op("st", lambda e: e.dma_start(out=dbg_aps["ao"],
                                         in_=AR[0:64, AO0:AO0 + 4096]),
             reads=list(bao))
    if "negc" in dbg_aps:
        P.op("st", lambda e: e.dma_start(out=dbg_aps["negc"], in_=negc_cols[0]),
             reads=[bnegc[0]])


    if "lrow" in dbg_aps:
        P.op("st", lambda e: e.dma_start(out=dbg_aps["lrow"],
                                         in_=fsc[64:65, 0:512]),
             reads=[brow] + blinv)

    for t in range(4 if stages >= 4 else 0):
        c0 = t * 512
        ln_tile(lambda c, cc, n, _c0=c0: xs[c][:, _c0 + cc:_c0 + cc + n], bxs,
                lambda c, cc, n, _c0=c0: colA_v(c, _c0 + cc, n), bcolA, 512, t % 2)

    def emit_gate(t):
        c0 = t * 512
        for k in range(DC):
            P.op("pe", lambda e, _k=k, _c0=c0: e.matmul(
                pb[6][0:9, 0:512], wgt_v(_k), colA_v(_k, _c0, 512),
                start=(_k == 0), stop=(_k == DC - 1)),
                reads=[bwgt, bcolA[k]], writes=[bpb[6]])
        P.op("dve", lambda e: e.tensor_scalar(gsb, pb[6][0:9, 0:512], bgt_col,
                                              None, AluOpType.add),
             reads=[bpb[6], bbgt], writes=[bgsb, brow2])
        for u in range(4):
            u0 = u * 128
            P.op("pe", lambda e, _u0=u0: e.transpose(
                pb[4][:, 0:9], gsb[:, _u0:_u0 + 128], identf[0:9, 0:9]),
                reads=[bgsb, bidf], writes=[bpb[4]])
            P.op("dve", lambda e: e.reduce_max(g_negmax, pb[4][:, 0:8], axis=AX.X,
                                               negate=True),
                 reads=[bpb[4]], writes=[bgate])
            P.op("act", lambda e: e.activation(g_pg, pb[4][:, 0:8], AF.Exp,
                                               bias=g_negmax),
                 reads=[bpb[4], bgate], writes=[bgate])
            P.op("dve", lambda e: e.tensor_reduce(g_zi, g_pg, AX.X, AluOpType.add),
                 reads=[bgate], writes=[bgate])
            P.op("dve", lambda e: e.reciprocal(g_zi, g_zi),
                 reads=[bgate], writes=[bgate])
            P.op("dve", lambda e: e.tensor_scalar(g_pg, g_pg, g_zi, None,
                                                  AluOpType.mult),
                 reads=[bgate], writes=[bgate])
            P.op("act", lambda e: e.activation(g_thr, pb[4][:, 8:9], AF.Tanh,
                                               scale=0.5),
                 reads=[bpb[4]], writes=[bgate])
            P.op("dve", lambda e: e.tensor_scalar(g_thr, g_thr, 0.0625, 0.0625,
                                                  AluOpType.mult, AluOpType.add),
                 reads=[bgate], writes=[bgate])
            P.op("dve", lambda e: e.tensor_scalar(g_w, g_pg, g_thr, 0.0,
                                                  AluOpType.subtract,
                                                  AluOpType.max),
                 reads=[bgate], writes=[bgate])
            P.op("dve", lambda e: e.tensor_reduce(g_s, g_w, AX.X, AluOpType.add),
                 reads=[bgate], writes=[bgate])
            P.op("dve", lambda e: e.tensor_scalar(g_eq, g_s, 0.0, None,
                                                  AluOpType.is_equal),
                 reads=[bgate], writes=[bgate])
            P.op("dve", lambda e: e.tensor_tensor(g_s, g_s, g_eq, AluOpType.add),
                 reads=[bgate], writes=[bgate])
            P.op("dve", lambda e: e.reciprocal(g_s, g_s),
                 reads=[bgate], writes=[bgate])
            P.op("dve", lambda e: e.tensor_scalar(g_w, g_w, g_s, None,
                                                  AluOpType.mult),
                 reads=[bgate], writes=[bgate])
            P.op("pe", lambda e: e.transpose(pb[5][0:8, 0:128], g_w, identf),
                 reads=[bgate, bidf], writes=[bpb[5]])
            P.op("act", lambda e, _c0=c0, _u0=u0: e.activation(
                wfm[:, _c0 + _u0:_c0 + _u0 + 128], pb[5][0:8, 0:128], AF.Copy),
                reads=[bpb[5]], writes=[bwfm, bosc2])

    if stages >= 4:
        emit_gate(0)
        emit_gate(1)

    for e_ in range(E if stages >= 5 else 0):
        for ht in range(2):
            t0 = ht * 1024
            for fp2 in range(FC // 2):
                wpar = fp2 % 2
                P.op("wb1a" if wpar == 0 else "wb1b",
                     lambda e, _e=e_, _fp=fp2, _wp=wpar: e.dma_start(
                         out=wb1[_wp][:].rearrange("p (f c) -> p f c", f=2),
                         in_=w1t[_e, 2 * _fp:2 * _fp + 2].rearrange(
                             "f p c -> p f c")),
                     writes=[bwb1[wpar]])
                for fo in range(2):
                    f = 2 * fp2 + fo
                    for tt in range(2):
                        hpar = [0, 1, 6][(2 * f + tt) % 3]
                        for k in range(DC):
                            P.op("pe", lambda e, _fo=fo, _tt=tt, _k=k, _wp=wpar,
                                 _hp=hpar, _t0=t0: e.matmul(
                                     pb[_hp][:, 0:512],
                                     wb1[_wp][:, _fo * 1024 + _k * 128:
                                              _fo * 1024 + (_k + 1) * 128],
                                     colA_v(_k, _t0 + _tt * 512, 512),
                                     start=(_k == 0), stop=(_k == DC - 1)),
                                 reads=[bwb1[wpar], bcolA[k]], writes=[bpb[hpar]])
                        P.op("act", lambda e, _e=e_, _f=f, _tt=tt, _hp=hpar:
                             e.activation(
                                 hp_v(_f, _tt * 512, 512), pb[_hp][:, 0:512],
                                 AF.Gelu_apprx_tanh,
                                 bias=b1sb[:, _e * FC + _f:_e * FC + _f + 1]),
                             reads=[bpb[hpar], bb1] + (hp_alias[f]
                                                       if _ffn_first(e_, ht)
                                                       else []),
                             writes=[bhp[f]])
            if e_ == 0 and ht == 1:
                emit_gate(2)
                emit_gate(3)
            for tt in range(2):
                wr = fsc[64:65, 0:512] if tt == 0 else fsc[64:65, 512:1024]
                P.op("spm", lambda e, _e=e_, _t0=t0, _tt=tt, _wr=wr: e.dma_start(
                    out=_wr,
                    in_=wfm[_e:_e + 1,
                            _t0 + _tt * 512:_t0 + _tt * 512 + 512]),
                    reads=[bwfm], writes=[brow])
                P.op("pe", lambda e, _tt=tt, _wr=wr: e.matmul(
                    pb[4 + _tt][:, 0:512], onesblk[64:65, :], _wr,
                    start=True, stop=True),
                    reads=[brow, bonesf], writes=[bpb[4 + tt]])
                P.op("act", lambda e, _tt=tt: e.activation(
                    rep_v(_tt), pb[4 + _tt][:, 0:512], AF.Copy),
                    reads=[bpb[4 + tt]], writes=[brep[tt], brow2])
            for dd in range(DC):
                for fh in range(2):
                    wpar = fh
                    fam = "wb2a" if wpar == 0 else "wb2b"
                    P.op(fam, lambda e, _e=e_, _dd=dd, _fh=fh, _wp=wpar:
                         e.dma_start(
                             out=wb2[_wp][:],
                             in_=w2t[_e, _dd][:, _fh * 2048:(_fh + 1) * 2048]),
                         writes=[bwb2[wpar]])
                    for tt in range(2):
                        epar = 2 + (2 * dd + tt) % 4
                        for fc2 in range(16):
                            f = fh * 16 + fc2
                            P.op("pe", lambda e, _f=f, _fc=fc2, _wp=wpar, _tt=tt,
                                 _ep=epar, _t0=t0: e.matmul(
                                     pb[_ep][:, 0:512],
                                     wb2[_wp][:, _fc * 128:(_fc + 1) * 128],
                                     hp_v(_f, _tt * 512, 512),
                                     start=(_f == 0), stop=(_f == FC - 1)),
                                 reads=[bwb2[wpar], bhp[f]], writes=[bpb[epar]])
                for tt in range(2):
                    epar = 2 + (2 * dd + tt) % 4
                    P.op("dve", lambda e, _e=e_, _dd=dd, _tt=tt, _ep=epar:
                         e.scalar_tensor_tensor(
                             t1_v(512), pb[_ep][:, 0:512],
                             b2sb[:, _e * DC + _dd:_e * DC + _dd + 1],
                             rep_v(_tt), AluOpType.add, AluOpType.mult),
                         reads=[bpb[epar], bb2, brep[tt]], writes=[bt1])
                    P.op("dve", lambda e, _dd=dd, _tt=tt, _t0=t0: e.tensor_tensor(
                        xs[_dd][:, _t0 + _tt * 512:_t0 + _tt * 512 + 512],
                        xs[_dd][:, _t0 + _tt * 512:_t0 + _tt * 512 + 512],
                        t1_v(512), AluOpType.add),
                        reads=[bt1], writes=[bxs[dd]])

    for c in range(DC):
        P.op("st", lambda e, _c=c: e.dma_start(out=yT[_c * 128:(_c + 1) * 128, :],
                                               in_=xs[_c][:]),
             reads=[bxs[c]])
    if "kvf" in dbg_aps:
        P.op("st", lambda e: e.dma_start(out=dbg_aps["kvf"], in_=kvf[:]),
             reads=[bkvf])
    if "kvk" in dbg_aps:
        P.op("st", lambda e: e.dma_start(out=dbg_aps["kvk"],
                                         in_=AR[:, KVK0:KVK0 + KC * 65]),
             reads=[bkvk])
    if "xn2" in dbg_aps:
        P.op("st", lambda e: e.dma_start(out=dbg_aps["xn2"], in_=AR[:, 0:16384]),
             reads=list(bcolA))
    if "wfm" in dbg_aps:
        P.op("st", lambda e: e.dma_start(out=dbg_aps["wfm"], in_=wfm),
             reads=[bwfm])

    P.final_wait("sp", "st", P.cnt["st"])

    if not emit:
        stack.close()
        return P
    sem_stack = contextlib.ExitStack()
    sems = {}
    for name in ALL_ENGS:
        sems[name] = sem_stack.enter_context(nc.semaphore("s_" + name))
    with nc.Block() as block:

        @block.tensor
        def _(t):
            for fn in P.ops["pe"]:
                fn(t, sems)

        @block.scalar
        def _(t):
            for fn in P.ops["act"]:
                fn(t, sems)

        @block.vector
        def _(t):
            for fn in P.ops["dve"]:
                fn(t, sems)

        @block.gpsimd
        def _(t):
            for fn in P.ops["gp"]:
                fn(t, sems)

        @block.sync
        def _(t):
            for fn in P.ops["sp"]:
                fn(t, sems)

    sem_stack.close()
    stack.close()
    return P


def _ffn_first(e_, ht):
    return e_ == 0 and ht == 0


def build2(nc, dbg=(), stages=6):
    tmp = bass.Bass("TRN2", target_bir_lowering=False)
    P1 = build(tmp, dbg, stages, emit=False)
    return build(nc, dbg, stages, needed=P1.needed)


def _host_layout(inputs):
    x = np.asarray(inputs["L0_attn_x"], dtype=np.float32)
    Wq = np.asarray(inputs["L0_attn_Wq"], dtype=np.float32)
    Wkv = np.asarray(inputs["L0_attn_Wkv"], dtype=np.float32)
    null_kv = np.asarray(inputs["L0_attn_null_kv"], dtype=np.float32)
    Wout = np.asarray(inputs["L0_attn_Wout"], dtype=np.float32)
    Wg = np.asarray(inputs["L0_ffn_Wg"], dtype=np.float32)
    bg = np.asarray(inputs["L0_ffn_bg"], dtype=np.float32)
    Wt = np.asarray(inputs["L0_ffn_Wt"], dtype=np.float32)
    bt = np.asarray(inputs["L0_ffn_bt"], dtype=np.float32)
    W1 = np.asarray(inputs["L0_ffn_W1"], dtype=np.float32)
    b1 = np.asarray(inputs["L0_ffn_b1"], dtype=np.float32)
    W2 = np.asarray(inputs["L0_ffn_W2"], dtype=np.float32)
    b2 = np.asarray(inputs["L0_ffn_b2"], dtype=np.float32)

    shared = {
        "wq": np.ascontiguousarray(Wq),
        "wkv": np.ascontiguousarray(Wkv),
        "nullkv": np.ascontiguousarray(null_kv.reshape(DH, 1)),
        "woutH": np.ascontiguousarray(
            Wout.reshape(H, DH, D).transpose(1, 0, 2).reshape(DH, H * D)),
        "wgt": np.ascontiguousarray(np.concatenate([Wg, Wt], axis=1)),
        "bgt": np.ascontiguousarray(np.concatenate([bg, bt]).reshape(9, 1)),
        "w1t": np.ascontiguousarray(
            W1.reshape(E, 8, 128, FC, 128).transpose(0, 3, 2, 1, 4)
            .reshape(E, FC, 128, 1024)),
        "b1t": np.ascontiguousarray(
            b1.reshape(E, FC, 128).transpose(2, 0, 1).reshape(128, E * FC)),
        "w2t": np.ascontiguousarray(
            W2.reshape(E, FC, 128, DC, 128).transpose(0, 3, 2, 1, 4)
            .reshape(E, DC, 128, 4096)),
        "b2t": np.ascontiguousarray(
            b2.reshape(E, DC, 128).transpose(2, 0, 1).reshape(128, E * DC)),
    }
    jj = np.arange(128)[:, None]
    ii = np.arange(SL)[None, :]
    in_maps = []
    own_cols = []
    xTb_cache = {}
    for core in range(8):
        b = core % 4
        g = core // 4
        if b not in xTb_cache:
            xTb_cache[b] = np.ascontiguousarray(x[b].T)
        xTb = xTb_cache[b]
        cols = (np.arange(NSLOT)[:, None] * 512 + g * 256
                + np.arange(SL)[None, :]).reshape(-1)
        own_cols.append((b, cols))
        m = np.zeros((128, 1280), dtype=BF16NP)
        for r in range(5):
            mr = (jj <= ii + 1 + 256 * g - 128 * r).astype(BF16NP)
            m[:, r * SL:(r + 1) * SL] = mr
        im = dict(shared)
        im["xT"] = xTb
        im["xTo"] = np.ascontiguousarray(xTb[:, cols])
        im["masks"] = m
        in_maps.append(im)
    return in_maps, own_cols


def kernel(**inputs):
    in_maps, own_cols = _host_layout(inputs)
    nc = bass.Bass("TRN2", target_bir_lowering=False)
    build2(nc)
    res = run_bass_kernel_spmd(nc, in_maps, core_ids=list(range(8)))
    out = np.empty((B, N, D), dtype=np.float32)
    for core in range(8):
        b, cols = own_cols[core]
        out[b, cols, :] = res.results[core]["yT"].T
    return out


# version 14
# speedup vs baseline: 1.0190x; 1.0105x over previous
import contextlib
import numpy as np
import ml_dtypes
import concourse.bass as bass
import concourse.mybir as mybir
from concourse.alu_op_type import AluOpType
from concourse import library_config
from concourse.bass_utils import run_bass_kernel_spmd

F32 = mybir.dt.float32
BF16 = mybir.dt.bfloat16
AF = mybir.ActivationFunctionType
AX = mybir.AxisListType
BF16NP = ml_dtypes.bfloat16

B = 4
N = 4096
D = 1024
DC = 8
H = 16
DH = 64
E = 8
FF = 4096
FC = 32
TOWN = 2048
NSLOT = 8
SL = 256
KC = 33
KVN = KC * 128
LNEPS = 1e-5
QSCALE = DH ** -0.5
ATTN_SKEW = True
ATTN_NSLOT = NSLOT

DMA_FAMS = {"ld0": "gp", "wb1a": "gp", "wb1b": "gp", "wb2a": "gp",
            "wb2b": "gp", "spi": "sp", "osca": "sp", "oscb": "sp", "spm": "sp", "st": "sp"}
ALL_ENGS = ("pe", "act", "dve", "gp") + tuple(DMA_FAMS)


class Buf:
    __slots__ = ("name", "writer", "readers")

    def __init__(self, name):
        self.name = name
        self.writer = None
        self.readers = {}

    def __repr__(self):
        return f"Buf({self.name})"


class Prog:

    def __init__(self, needed=None):
        self.ops = {e: [] for e in ("pe", "act", "dve", "gp", "sp")}
        self.cnt = {e: 0 for e in ALL_ENGS}
        self.waited = {}
        self.drained = {e: 0 for e in ("pe", "act", "dve", "gp", "sp")}
        self.n_waits = 0
        self.n_drains = 0
        self.needed = needed if needed is not None else {e: set() for e in ALL_ENGS}
        self.pass2 = needed is not None
        self.semval = {e: 0 for e in ALL_ENGS}
        self.svmap = {e: {} for e in ALL_ENGS}
        self.n_incs = 0

    @staticmethod
    def _stream(eng):
        return DMA_FAMS.get(eng, eng)

    def op(self, eng, fn, reads=(), writes=(), tag=""):
        stream = self._stream(eng)
        is_dma = eng in DMA_FAMS
        waits = {}
        need_drain = False
        for b in tuple(reads) + tuple(writes):
            w = b.writer
            if w is not None:
                weng, wcnt = w
                if weng == eng and is_dma:
                    if waits.get(eng, 0) < self.cnt[eng]:
                        waits[eng] = self.cnt[eng]
                elif weng == eng and not is_dma:
                    if eng != "pe" and wcnt > self.drained[eng]:
                        need_drain = True
                elif weng != eng:
                    v = self.cnt[weng] if weng in DMA_FAMS else wcnt
                    if waits.get(weng, 0) < v:
                        waits[weng] = v
        for b in writes:
            for reng, rcnt in b.readers.items():
                if reng == eng and not is_dma:
                    if eng != "pe" and rcnt > self.drained[eng]:
                        need_drain = True
                elif reng != eng:
                    v = self.cnt[reng] if reng in DMA_FAMS else rcnt
                    if waits.get(reng, 0) < v:
                        waits[reng] = v
        final_waits = []
        for peng, val in waits.items():
            key = (stream, peng)
            if self.waited.get(key, 0) < val:
                self.waited[key] = val
                final_waits.append((peng, val))
                if not self.pass2:
                    self.needed[peng].add(val)
        inc = 16 if is_dma else 1
        self.cnt[eng] += inc
        my = self.cnt[eng]
        self.n_waits += len(final_waits)
        if need_drain:
            self.n_drains += 1
            self.drained[eng] = my - inc

        if self.pass2:
            do_inc = is_dma or my in self.needed[eng]
            if do_inc:
                self.semval[eng] += inc
                self.svmap[eng][my] = self.semval[eng]
                self.n_incs += 1
            tw = tuple((p, v if p in DMA_FAMS else self.svmap[p][v])
                       for p, v in final_waits)
        else:
            do_inc = True
            self.semval[eng] += inc
            self.svmap[eng][my] = self.semval[eng]
            tw = tuple(final_waits)

        def thunk(e, sems, _w=tw, _d=need_drain, _fn=fn,
                  _inc=inc, _eng=eng, _di=do_inc):
            for peng, val in _w:
                e.wait_ge(sems[peng], val)
            if _d:
                e.drain()
            inst = _fn(e)
            if _di:
                inst.then_inc(sems[_eng], _inc)

        self.ops[stream].append(thunk)
        for b in writes:
            b.writer = (eng, my)
            b.readers = {}
        for b in reads:
            b.readers[eng] = my
        return my

    def raw(self, stream, fn):
        self.ops[stream].append(lambda e, sems, _fn=fn: _fn(e))

    def final_wait(self, stream, eng, val):
        if not self.pass2:
            self.needed[eng].add(val)
        v = val if eng in DMA_FAMS else self.svmap[eng].get(val, val)
        self.ops[stream].append(
            lambda e, sems, _eng=eng, _v=v: e.wait_ge(sems[_eng], _v))


def build(nc, dbg=(), stages=6, needed=None, emit=True):
    P = Prog(needed)

    xT = nc.dram_tensor("xT", [D, N], F32, kind="ExternalInput").ap()
    xTo = nc.dram_tensor("xTo", [D, TOWN], F32, kind="ExternalInput").ap()
    wq = nc.dram_tensor("wq", [D, D], F32, kind="ExternalInput").ap()
    wkv = nc.dram_tensor("wkv", [D, DH], F32, kind="ExternalInput").ap()
    nullkv = nc.dram_tensor("nullkv", [DH, 1], F32, kind="ExternalInput").ap()
    woutH = nc.dram_tensor("woutH", [DH, H * D], F32, kind="ExternalInput").ap()
    wgt = nc.dram_tensor("wgt", [D, 9], F32, kind="ExternalInput").ap()
    bgt = nc.dram_tensor("bgt", [9, 1], F32, kind="ExternalInput").ap()
    w1t = nc.dram_tensor("w1t", [E, FC, 128, 1024], F32, kind="ExternalInput").ap()
    b1t = nc.dram_tensor("b1t", [128, E * FC], F32, kind="ExternalInput").ap()
    w2t = nc.dram_tensor("w2t", [E, DC, 128, 4096], F32, kind="ExternalInput").ap()
    b2t = nc.dram_tensor("b2t", [128, E * DC], F32, kind="ExternalInput").ap()
    masks = nc.dram_tensor("masks", [128, 1280], BF16, kind="ExternalInput").ap()
    yT = nc.dram_tensor("yT", [D, TOWN], F32, kind="ExternalOutput").ap()

    dbg_aps = {}
    for name, shape, dt in (("kvf", [128, KVN], BF16),
                            ("qb", [128, 16384], BF16),
                            ("xn2", [128, 16384], BF16),
                            ("wfm", [8, 2048], F32),
                            ("xattn", [D, TOWN], F32),
                            ("kvk", [128, KC * 65], BF16),
                            ("ao", [64, 4096], BF16),
                            ("negc", [128, 1], F32),
                            ("mk", [128, 1280], BF16),
                            ("pp", [128, 2 * SL], BF16),
                            ("lrow", [1, 2 * SL], F32)):
        if name in dbg:
            dbg_aps[name] = nc.dram_tensor("dbg_" + name, shape, dt,
                                           kind="ExternalOutput").ap()

    stack = contextlib.ExitStack()
    sbt = lambda name, shape, dt: stack.enter_context(nc.sbuf_tensor(name, shape, dt))
    pst = lambda name, shape, dt: stack.enter_context(nc.psum_tensor(name, shape, dt))

    xs = [sbt(f"xs{c}", [128, TOWN], F32) for c in range(DC)]
    AR = sbt("arena", [128, 49152], BF16)
    kvf = sbt("kvf", [128, KVN], BF16)
    wb1 = [sbt(f"wb1_{i}", [128, 2048], BF16) for i in range(2)]
    wb2 = [sbt(f"wb2_{i}", [128, 2048], BF16) for i in range(2)]
    fsc = sbt("fsc", [128, 5248], F32)
    wgts = sbt("wgts", [128, 72], BF16)

    def colA_v(c, c0, n):
        return AR[:, c * 2048 + c0: c * 2048 + c0 + n]

    WQ0 = 16384
    def wq_v(k, j0, n):
        return AR[:, WQ0 + k * 1024 + j0: WQ0 + k * 1024 + j0 + n]

    def woutH_v(h, j0, n):
        return AR[0:64, WQ0 + h * 1024 + j0: WQ0 + h * 1024 + j0 + n]

    XN0 = 32768
    def xnk_v(c, n=256, par=0):
        return AR[:, XN0 + par * 2048 + c * 256: XN0 + par * 2048 + c * 256 + n]

    def xnq_v(c, c0, n):
        return AR[:, XN0 + c * 512 + c0: XN0 + c * 512 + c0 + n]

    AO0 = 36864
    def ao_v(h):
        return AR[0:64, AO0 + h * SL: AO0 + (h + 1) * SL]

    MK0 = 40960
    def maskp_v(j, n=512):
        return AR[:, MK0 + j * 512: MK0 + j * 512 + n]

    PB0 = 42240
    def p2_v(hf, par):
        i = 2 * hf + par
        return AR[:, PB0 + i * 512: PB0 + (i + 1) * 512]

    SQ0 = 44544
    def sqb_v(par, n=512):
        return AR[:, SQ0 + par * 512: SQ0 + par * 512 + n]

    ones_b = AR[:, 45568:45569]
    WKV0 = 45584
    def wkv_v(k):
        return AR[:, WKV0 + k * 64: WKV0 + (k + 1) * 64]

    def wgt_v(k):
        return wgts[:, k * 9:(k + 1) * 9]

    IDB0 = 46176
    identb = AR[:, IDB0:IDB0 + 128]
    KVK0 = 46336
    kvk = AR[:, KVK0:KVK0 + KC * 65 + 63]

    def hp_v(fc, c0, n):
        return AR[:, 16384 + fc * 1024 + c0: 16384 + fc * 1024 + c0 + n]

    row_m = fsc[0:1, 0:512]
    row_ms = fsc[32:33, 0:512]
    row_var = fsc[64:65, 0:512]
    row_tmp = fsc[96:97, 0:512]
    row_rstd = fsc[0:1, 512:1024]
    row_mr = fsc[32:33, 512:1024]
    wrow = fsc[96:97, 512:1024]
    row_kn2 = fsc[64:65, 512:513]
    row_sc1 = fsc[64:65, 513:514]
    row_sc2 = fsc[64:65, 514:515]
    row_linv = [fsc[64:65, 0:256], fsc[64:65, 256:512]]
    ones_f = fsc[:, 1024:1025]
    negc_cols = [fsc[:, 1025:1026], fsc[:, 1033:1034]]
    eps_col = fsc[:, 1026:1027]
    g_negmax = fsc[:, 1027:1028]
    g_zi = fsc[:, 1028:1029]
    g_thr = fsc[:, 1029:1030]
    g_s = fsc[:, 1030:1031]
    g_eq = fsc[:, 1031:1032]
    bgt_col = fsc[0:9, 1032:1033]
    g_pg = fsc[:, 1036:1044]
    g_w = fsc[:, 1044:1052]
    b1sb = fsc[:, 1056:1312]
    identf = fsc[:, 1312:1440]
    b2sb = fsc[:, 1440:1504]
    REP = (1536, 2048)
    def rep_v(i, n=512):
        return fsc[:, REP[i]:REP[i] + n]
    gsb = fsc[0:9, 1536:2048]
    T10 = 2560
    def t1_v(n=512):
        return fsc[:, T10:T10 + n]
    def t1p_v(p0, p1, n):
        return fsc[p0:p1, T10:T10 + n]
    def lrep_v(par):
        return fsc[0:64, T10 + par * SL: T10 + (par + 1) * SL]
    wfm = fsc[0:8, 3072:5120]
    def osc_v(c, c0, n):
        return fsc[:, 3072 + c * 256 + c0: 3072 + c * 256 + c0 + n]
    onesblk = fsc[:, 5120:5248]

    pb = [pst(f"pb{i}", [128, 512], F32) for i in range(7)]
    pbT = pst("pbT", [128, 1024], BF16)

    bxs = [Buf(f"xs{c}") for c in range(DC)]
    bcolA = [Buf(f"colA{c}") for c in range(DC)]
    bkvf = Buf("kvf")
    bkvk = Buf("kvk")
    bwb1 = [Buf(f"wb1_{i}") for i in range(2)]
    bwb2 = [Buf(f"wb2_{i}") for i in range(2)]
    bwq = Buf("wq_wout")
    bxn = [Buf(f"xn{c}") for c in range(DC)]
    bao = [Buf(f"ao{h}") for h in range(H)]
    bmask = Buf("mask")
    bp = [Buf(f"p{i}") for i in range(4)]
    bsqb = [Buf(f"sqb{i}") for i in range(2)]
    bonesb = Buf("onesb")
    bonesf = Buf("onesf")
    bidf = Buf("identf")
    bidb = Buf("identb")
    bwkv = Buf("wkv")
    bwgt = Buf("wgt")
    bosc = Buf("osc")
    brow = Buf("rows")
    bnegc = [Buf("negc0"), Buf("negc1")]
    blinv = [Buf(f"linv{i}") for i in range(2)]
    brep = [Buf(f"rep{i}") for i in range(2)]
    bb1 = Buf("b1")
    bb2 = Buf("b2")
    bbgt = Buf("bgt")
    bgate = Buf("gatescratch")
    bgsb = brep[0]
    bwfm = Buf("wfm")
    bhp = [Buf(f"hp{fc}") for fc in range(FC)]
    bpb = [Buf(f"pb{i}") for i in range(7)]
    bpbT = Buf("pbT")
    bkn2 = brow
    bt1 = Buf("t1")
    blrep = [bt1, bt1]

    hp_alias = {fc: [] for fc in range(FC)}
    for fc in range(16):
        hp_alias[fc].append(bwq)
    for fc in range(16, 20):
        hp_alias[fc].extend(bxn)
    for fc in range(20, 24):
        hp_alias[fc].extend(bao)
    for fc in range(24, 32):
        hp_alias[fc].extend([bmask, bp[0], bp[1], bsqb[0], bsqb[1], bonesb,
                             bwkv, bidb, bkvk])

    wq_r = wq.rearrange("(k p) d -> p k d", p=128)
    P.op("ld0", lambda e: e.dma_start(
        out=AR[:, WQ0:WQ0 + 8192].rearrange("p (k d) -> p k d", k=8), in_=wq_r),
        writes=[bwq])
    wkv_r = wkv.rearrange("(k p) d -> p k d", p=128)
    P.op("ld0", lambda e: e.dma_start(
        out=AR[:, WKV0:WKV0 + 512].rearrange("p (k d) -> p k d", k=8), in_=wkv_r),
        writes=[bwkv])
    wgt_r = wgt.rearrange("(k p) d -> p k d", p=128)
    P.op("ld0", lambda e: e.dma_start(
        out=wgts[:].rearrange("p (k d) -> p k d", k=8), in_=wgt_r),
        writes=[bwgt])
    P.op("ld0", lambda e: e.dma_start(out=kvf[0:64, 0:1], in_=nullkv), writes=[bkvf])

    P.op("spi", lambda e: e.dma_start(out=b1sb, in_=b1t), writes=[bb1])
    P.op("spi", lambda e: e.dma_start(out=b2sb, in_=b2t), writes=[bb2])
    P.op("spi", lambda e: e.dma_start(out=bgt_col, in_=bgt), writes=[bbgt])
    P.op("spi", lambda e: e.dma_start(out=AR[:, MK0:MK0 + 1280], in_=masks),
         writes=[bmask])
    xTo_r = xTo.rearrange("(c p) t -> p c t", p=128)
    for c in range(DC):
        P.op("spi", lambda e, _c=c: e.dma_start(out=xs[_c][:], in_=xTo_r[:, _c, :]),
             writes=[bxs[c]])

    P.op("gp", lambda e: e.memset(kvf[:, 1 + N:KVN], 0.0), writes=[bkvf])
    kvk3 = AR[:, KVK0:KVK0 + KC * 65].rearrange("p (c w) -> p c w", w=65)
    P.op("gp", lambda e: e.memset(kvk3[:, :, 64:65], 1.0), writes=[bkvk])
    P.op("gp", lambda e: e.memset(AR[:, KVK0 + KC * 65:KVK0 + KC * 65 + 63], 0.0),
         writes=[bkvk])
    def kvkp_v(kc):
        return AR[:, KVK0 + kc * 65: KVK0 + kc * 65 + 128]
    P.op("gp", lambda e: e.iota(fsc[:, 1312:1440], pattern=[[-1, 128]], base=0,
                                channel_multiplier=1,
                                allow_small_or_imprecise_dtypes=True),
         writes=[bidf])
    P.op("dve", lambda e: e.tensor_scalar(identf, fsc[:, 1312:1440], 0.0, None,
                                          AluOpType.is_equal),
         reads=[bidf], writes=[bidf])
    P.op("dve", lambda e: e.tensor_copy(identb, identf), reads=[bidf],
         writes=[bidb])
    P.op("gp", lambda e: e.memset(ones_f, 1.0), writes=[bonesf])
    P.op("gp", lambda e: e.memset(onesblk, 1.0), writes=[bonesf])
    P.op("gp", lambda e: e.memset(eps_col, LNEPS), writes=[bonesf])
    P.op("gp", lambda e: e.memset(ones_b, 1.0), writes=[bonesb])
    P.op("gp", lambda e: e.memset(row_kn2, 0.0), writes=[bkn2])

    LNS = [dict(m=row_m, ms=row_ms, var=row_var, tmp=row_tmp,
                rstd=row_rstd, mr=row_mr, sS=4, sSS=5,
                t1=lambda n: t1_v(n)),
           dict(m=fsc[0:1, 1536:2048], ms=fsc[32:33, 1536:2048],
                var=fsc[64:65, 1536:2048], tmp=fsc[96:97, 1536:2048],
                rstd=fsc[0:1, 2048:2560], mr=fsc[32:33, 2048:2560],
                sS=0, sSS=1,
                t1=lambda n: t1_v(n))]
    brow2 = Buf("rows2")
    LNB = [dict(row=brow, t1=bt1), dict(row=brow2, t1=bt1)]

    def ln_tile(src_v, src_bufs, out_v, out_bufs, n, par, sset=0):
        S = LNS[sset]
        rbuf, tbuf = LNB[sset]["row"], LNB[sset]["t1"]
        psS, bS = pb[S["sS"]], bpb[S["sS"]]
        psSS, bSS = pb[S["sSS"]], bpb[S["sSS"]]
        for c in range(DC):
            P.op("pe", lambda e, _c=c: e.matmul(
                psS[0:1, 0:n], ones_f, src_v(_c, 0, n),
                start=(_c == 0), stop=(_c == DC - 1)),
                reads=[bonesf, src_bufs[c]], writes=[bS])
        for c in range(DC):
            P.op("dve", lambda e, _c=c: e.tensor_tensor(
                sqb_v(par, n), src_v(_c, 0, n), src_v(_c, 0, n), AluOpType.mult),
                reads=[src_bufs[c]], writes=[bsqb[par]])
            P.op("pe", lambda e, _c=c: e.matmul(
                psSS[0:1, 0:n], ones_b, sqb_v(par, n),
                start=(_c == 0), stop=(_c == DC - 1)),
                reads=[bonesb, bsqb[par]], writes=[bSS])
        P.op("act", lambda e: e.activation(S["m"][:, 0:n], psS[0:1, 0:n], AF.Copy,
                                           scale=1.0 / D),
             reads=[bS], writes=[rbuf])
        P.op("act", lambda e: e.activation(S["ms"][:, 0:n], psS[0:1, 0:n], AF.Square,
                                           scale=1.0 / D),
             reads=[bS], writes=[rbuf])
        P.op("dve", lambda e: e.scalar_tensor_tensor(
            S["var"][:, 0:n], psSS[0:1, 0:n], 1.0 / D, S["ms"][:, 0:n],
            AluOpType.mult, AluOpType.subtract),
             reads=[bSS, rbuf], writes=[rbuf])
        P.op("act", lambda e: e.activation(S["tmp"][:, 0:n], S["var"][:, 0:n], AF.Ln,
                                           bias=eps_col[0:1, :]),
             reads=[rbuf, bonesf], writes=[rbuf])
        P.op("act", lambda e: e.activation(S["rstd"][:, 0:n], S["tmp"][:, 0:n], AF.Exp,
                                           scale=-0.5),
             reads=[rbuf], writes=[rbuf])
        P.op("dve", lambda e: e.tensor_tensor(S["mr"][:, 0:n], S["m"][:, 0:n],
                                              S["rstd"][:, 0:n], AluOpType.mult),
             reads=[rbuf], writes=[rbuf])
        P.op("pe", lambda e: e.matmul(pb[3][:, 0:n], onesblk[0:1, :],
                                      S["rstd"][:, 0:n], start=True, stop=True),
             reads=[rbuf, bonesf], writes=[bpb[3]])
        P.op("pe", lambda e: e.matmul(pb[2][:, 0:n], onesblk[32:33, :],
                                      S["mr"][:, 0:n], start=True, stop=True),
             reads=[rbuf, bonesf], writes=[bpb[2]])
        for c in range(DC):
            P.op("dve", lambda e, _c=c: e.tensor_tensor(
                S["t1"](n), src_v(_c, 0, n), pb[3][:, 0:n], AluOpType.mult),
                reads=[src_bufs[c], bpb[3]], writes=[tbuf])
            P.op("dve", lambda e, _c=c: e.tensor_tensor(
                out_v(_c, 0, n), S["t1"](n), pb[2][:, 0:n], AluOpType.subtract),
                reads=[tbuf, bpb[2]], writes=[out_bufs[c]])

    xT_r = xT.rearrange("(c p) t -> p c t", p=128)
    xnk_bufs = [bxn[c // 2] for c in range(DC)]
    bosc2 = Buf("osc2")

    def emit_kv_dma(blk):
        P.op("osca" if blk % 2 == 0 else "oscb",
             lambda e, _b=blk: e.dma_start(
                 out=fsc[:, 3072:5120].rearrange("p (c t) -> p c t", c=8),
                 in_=xT_r[:, :, _b * 256:(_b + 1) * 256]),
             writes=[bosc2])

    def emit_kv_ln_rest(blk):
        par = blk % 2
        ln_tile(lambda c, c0, n: osc_v(c, c0, n), [bosc2] * DC,
                lambda c, c0, n, _p=par: xnk_v(c, n, _p),
                [bxn[2 * par + c // 4] for c in range(DC)], 256, par,
                sset=par)

    def emit_kv_ln(blk):
        emit_kv_dma(blk)
        emit_kv_ln_rest(blk)

    def emit_kv_tail(blk):
        par = blk % 2
        for c in range(DC):
            P.op("pe", lambda e, _c=c, _p=par: e.matmul(
                pb[6][0:64, 0:256], wkv_v(_c), xnk_v(_c, 256, _p),
                start=(_c == 0), stop=(_c == DC - 1)),
                reads=[bwkv, bxn[2 * par], bxn[2 * par + 1]], writes=[bpb[6]])
        col = 1 + blk * 256
        P.op("act", lambda e, _col=col: e.activation(
            kvf[0:64, _col:_col + 256], pb[6][0:64, 0:256], AF.Copy),
            reads=[bpb[6]], writes=[bkvf])
        P.op("act", lambda e: e.activation(t1p_v(0, 64, 256), pb[6][0:64, 0:256],
                                           AF.Square),
             reads=[bpb[6]], writes=[bt1])
        P.op("pe", lambda e: e.matmul(pb[6][64:65, 0:256], ones_f[0:64, :],
                                      t1p_v(0, 64, 256), start=True, stop=True),
             reads=[bonesf, bt1], writes=[bpb[6]])
        P.op("dve", lambda e: e.reduce_max(row_sc1, pb[6][64:65, 0:256],
                                           axis=AX.X),
             reads=[bpb[6]], writes=[brow])
        P.op("dve", lambda e: e.tensor_tensor(row_kn2, row_kn2, row_sc1,
                                              AluOpType.max),
             reads=[brow], writes=[bkn2])
        P.op("spm", lambda e, _col=col: e.dma_start(
            out=kvf[64:128, _col:_col + 256], in_=kvf[0:64, _col:_col + 256]),
            reads=[bkvf], writes=[bkvf])
        chunks = [2 * blk, 2 * blk + 1] + ([32] if blk == 15 else [])
        for k in chunks:
            cpar = k % 4
            P.op("pe", lambda e, _k=k, _p=cpar: e.transpose(
                pbT[0:128, _p * 256:_p * 256 + 64],
                kvf[0:64, _k * 128:(_k + 1) * 128], identb[0:64, 0:64]),
                reads=[bkvf, bidb], writes=[bpbT])
            P.op("act", lambda e, _k=k, _p=cpar: e.activation(
                kvk3[:, _k, 0:64], pbT[0:128, _p * 256:_p * 256 + 64], AF.Copy),
                reads=[bpbT], writes=[bkvk])

    def emit_kv_step(blk):
        emit_kv_ln(blk)
        emit_kv_tail(blk)

    P.op("ld0", lambda e: e.dma_start(out=kvf[64:128, 0:1], in_=nullkv),
         writes=[bkvf])
    P.op("act", lambda e: e.activation(t1p_v(0, 64, 1), kvf[0:64, 0:1], AF.Square),
         reads=[bkvf], writes=[bt1])
    P.op("pe", lambda e: e.matmul(pb[6][64:65, 0:1], ones_f[0:64, :],
                                  t1p_v(0, 64, 1), start=True, stop=True),
         reads=[bonesf, bt1], writes=[bpb[6]])
    P.op("dve", lambda e: e.reduce_max(row_sc1, pb[6][64:65, 0:1], axis=AX.X),
         reads=[bpb[6]], writes=[brow])
    P.op("dve", lambda e: e.tensor_tensor(row_kn2, row_kn2, row_sc1, AluOpType.max),
         reads=[brow], writes=[bkn2])

    for blk in range(5):
        emit_kv_step(blk)
    _prologue_dma5 = True

    for t in range(4):
        c0 = t * 512
        ln_tile(lambda c, cc, n, _c0=c0: xs[c][:, _c0 + cc:_c0 + cc + n], bxs,
                lambda c, cc, n: xnq_v(c, cc, n), bxn, 512, t % 2)
        for qc in range(DC):
            ppar = qc % 2
            for k in range(DC):
                P.op("pe", lambda e, _qc=qc, _k=k, _pp=ppar: e.matmul(
                    pb[_pp][:, 0:512], wq_v(_k, _qc * 128, 128), xnq_v(_k, 0, 512),
                    start=(_k == 0), stop=(_k == DC - 1)),
                    reads=[bwq, bxn[k]], writes=[bpb[ppar]])
            P.op("act", lambda e, _qc=qc, _pp=ppar, _c0=c0: e.activation(
                colA_v(_qc, _c0, 512), pb[_pp][:, 0:512], AF.Copy, scale=QSCALE),
                reads=[bpb[ppar]], writes=[bcolA[qc]])

    if "qb" in dbg_aps:
        P.op("st", lambda e: e.dma_start(out=dbg_aps["qb"], in_=AR[:, 0:16384]),
             reads=list(bcolA))

    P.op("ld0", lambda e: e.dma_start(out=AR[0:64, WQ0:WQ0 + 16384], in_=woutH),
         reads=[bwq], writes=[bwq])
    if stages >= 3:
        emit_kv_dma(5)

    def emit_negc(s):
        if s >= NSLOT:
            return
        scol = s * SL
        par = s % 2
        for qc in range(DC):
            P.op("dve", lambda e, _qc=qc, _sc=scol: e.tensor_tensor(
                sqb_v(_qc % 2, SL), colA_v(_qc, _sc, SL), colA_v(_qc, _sc, SL),
                AluOpType.mult),
                reads=[bcolA[qc]], writes=[bsqb[qc % 2]])
            P.op("pe", lambda e, _qc=qc: e.matmul(
                pb[6][0:1, 0:SL], ones_b, sqb_v(_qc % 2, SL),
                start=(_qc == 0), stop=(_qc == DC - 1)),
                reads=[bonesb, bsqb[qc % 2]], writes=[bpb[6]])
        P.op("dve", lambda e: e.reduce_max(row_sc1, pb[6][0:1, 0:SL], axis=AX.X),
             reads=[bpb[6]], writes=[brow])
        P.op("dve", lambda e: e.tensor_tensor(row_sc2, row_sc1, row_kn2,
                                              AluOpType.mult),
             reads=[brow], writes=[brow])
        P.op("act", lambda e: e.activation(row_sc1, row_sc2, AF.Ln),
             reads=[brow], writes=[brow])
        P.op("act", lambda e: e.activation(row_sc2, row_sc1, AF.Exp, scale=0.5),
             reads=[brow], writes=[brow])
        P.op("act", lambda e: e.mul(row_sc1, row_sc2, -1.0),
             reads=[brow], writes=[brow])
        P.op("pe", lambda e: e.matmul(pb[6][0:128, 2:3], onesblk[64:65, :],
                                      row_sc1, start=True, stop=True),
             reads=[brow, bonesf], writes=[bpb[6]])
        P.op("act", lambda e, _p=par: e.activation(negc_cols[_p],
                                                   pb[6][0:128, 2:3], AF.Copy),
             reads=[bpb[6]], writes=[bnegc[par]])

    if stages >= 3:
        emit_negc(0)
    for s in range(ATTN_NSLOT if stages >= 3 else 0):
        if s >= 1 and 2 * s + 3 <= 15:
            emit_kv_ln_rest(2 * s + 3)
        scol = s * SL
        Cs = 4 * s + 5
        KP = (Cs + 1) // 2
        pend = []

        def flush_pend():
            for f_ in pend:
                f_()
            pend.clear()

        for hp in range(8):
            qc = hp
            psOs = (pb[4], pb[5])
            bOs = (bpb[4], bpb[5])

            def emit_sims(kp, _qc=qc):
                kcs = [2 * kp] + ([2 * kp + 1] if 2 * kp + 1 < Cs else [])
                for hf in range(2):
                    sp = 2 * hf + (kp % 2)
                    for u, kc in enumerate(kcs):
                        P.op("pe", lambda e, _kc=kc, _sp=sp, _q=_qc, _hf=hf,
                             _u=u, _sc=scol: e.matmul(
                                 pb[_sp][:, _u * 256:_u * 256 + 256],
                                 kvf[64 * _hf:64 * (_hf + 1),
                                     _kc * 128:(_kc + 1) * 128],
                                 AR[64 * _hf:64 * (_hf + 1),
                                    _q * 2048 + _sc:_q * 2048 + _sc + SL],
                                 start=True, stop=True),
                             reads=[bkvf, bcolA[_qc]], writes=[bpb[sp]])

            emit_sims(0)
            flush_pend()
            for kp in range(KP):
                if kp + 1 < KP:
                    emit_sims(kp + 1)
                kcs = [2 * kp] + ([2 * kp + 1] if 2 * kp + 1 < Cs else [])
                w = 256 * len(kcs)
                pp = kp % 2
                j = kp - 2 * s
                for hf in range(2):
                    sp = 2 * hf + pp
                    P.op("act", lambda e, _sp=sp, _pp=pp, _hf=hf, _w=w,
                         _np=s % 2: e.activation(
                             p2_v(_hf, _pp)[:, 0:_w], pb[_sp][:, 0:_w],
                             AF.Exp, bias=negc_cols[_np]),
                         reads=[bpb[sp], bnegc[s % 2]], writes=[bp[2 * hf + pp]])
                    if 0 <= j <= 2:
                        P.op("dve", lambda e, _hf=hf, _pp=pp, _j=j, _w=w:
                             e.tensor_tensor(
                                 p2_v(_hf, _pp)[:, 0:_w],
                                 p2_v(_hf, _pp)[:, 0:_w],
                                 maskp_v(_j, _w), AluOpType.mult),
                             reads=[bmask], writes=[bp[2 * hf + pp]])
                    for u, kc in enumerate(kcs):
                        P.op("pe", lambda e, _kc=kc, _pp=pp, _ps=psOs[hf],
                             _C=Cs, _hf=hf, _u=u: e.matmul(
                                 _ps[0:128, 0:SL], kvkp_v(_kc),
                                 p2_v(_hf, _pp)[:, _u * 256:_u * 256 + 256],
                                 start=(_kc == 0), stop=(_kc == _C - 1)),
                             reads=[bkvk, bp[2 * hf + pp]], writes=[bOs[hf]])

            def emit_evac(_hp=hp, _psOs=psOs, _bOs=bOs):
                for hf in range(2):
                    P.op("dve", lambda e, _ps=_psOs[hf], _lp=hf: e.reciprocal(
                        row_linv[_lp], _ps[64:65, 0:SL]),
                        reads=[_bOs[hf]], writes=[blinv[hf]])
                    P.op("pe", lambda e, _lp=hf: e.matmul(
                        pb[6][0:64, _lp * 256:_lp * 256 + 256],
                        onesblk[64:65, 0:64], row_linv[_lp],
                        start=True, stop=True),
                        reads=[blinv[hf], bonesf], writes=[bpb[6]])
                    P.op("act", lambda e, _lp=hf: e.activation(
                        lrep_v(_lp), pb[6][0:64, _lp * 256:_lp * 256 + 256],
                        AF.Copy),
                        reads=[bpb[6]], writes=[blrep[hf]])
                    P.op("dve", lambda e, _ps=_psOs[hf], _lp=hf,
                         _h=2 * _hp + hf: e.tensor_tensor(
                            ao_v(_h), _ps[0:64, 0:SL], lrep_v(_lp),
                            AluOpType.mult),
                        reads=[_bOs[hf], blrep[hf]], writes=[bao[2 * _hp + hf]])

            pend.append(emit_evac)
            if s >= 1:
                if hp == 2 and 2 * s + 3 <= 15:
                    flush_pend()
                    emit_kv_tail(2 * s + 3)
                elif hp == 3 and 2 * s + 4 <= 15:
                    flush_pend()
                    emit_kv_dma(2 * s + 4)
                elif hp == 5 and 2 * s + 4 <= 15:
                    flush_pend()
                    emit_kv_ln_rest(2 * s + 4)
                elif hp == 7 and 2 * s + 4 <= 15:
                    flush_pend()
                    emit_kv_tail(2 * s + 4)
        emit_negc(s + 1)
        if 2 * s + 5 <= 15:
            emit_kv_dma(2 * s + 5)
        flush_pend()
        for dd in range(DC):
            ppar = 4 + dd % 2
            for h in range(H):
                P.op("pe", lambda e, _dd=dd, _h=h, _pp=ppar: e.matmul(
                    pb[_pp][:, 0:SL], woutH_v(_h, _dd * 128, 128), ao_v(_h),
                    start=(_h == 0), stop=(_h == H - 1)),
                    reads=[bwq, bao[h]], writes=[bpb[ppar]])
            P.op("dve", lambda e, _dd=dd, _pp=ppar, _sc=scol: e.tensor_tensor(
                xs[_dd][:, _sc:_sc + SL], xs[_dd][:, _sc:_sc + SL],
                pb[_pp][:, 0:SL], AluOpType.add),
                reads=[bpb[ppar]], writes=[bxs[dd]])

    if "xattn" in dbg_aps:
        for c in range(DC):
            P.op("st", lambda e, _c=c: e.dma_start(
                out=dbg_aps["xattn"][_c * 128:(_c + 1) * 128, :], in_=xs[_c][:]),
                reads=[bxs[c]])
    if "ao" in dbg_aps:
        P.op("st", lambda e: e.dma_start(out=dbg_aps["ao"],
                                         in_=AR[0:64, AO0:AO0 + 4096]),
             reads=list(bao))
    if "negc" in dbg_aps:
        P.op("st", lambda e: e.dma_start(out=dbg_aps["negc"], in_=negc_cols[0]),
             reads=[bnegc[0]])


    if "lrow" in dbg_aps:
        P.op("st", lambda e: e.dma_start(out=dbg_aps["lrow"],
                                         in_=fsc[64:65, 0:512]),
             reads=[brow] + blinv)

    for t in range(4 if stages >= 4 else 0):
        c0 = t * 512
        ln_tile(lambda c, cc, n, _c0=c0: xs[c][:, _c0 + cc:_c0 + cc + n], bxs,
                lambda c, cc, n, _c0=c0: colA_v(c, _c0 + cc, n), bcolA, 512, t % 2)

    def emit_gate(t):
        c0 = t * 512
        for k in range(DC):
            P.op("pe", lambda e, _k=k, _c0=c0: e.matmul(
                pb[6][0:9, 0:512], wgt_v(_k), colA_v(_k, _c0, 512),
                start=(_k == 0), stop=(_k == DC - 1)),
                reads=[bwgt, bcolA[k]], writes=[bpb[6]])
        P.op("dve", lambda e: e.tensor_scalar(gsb, pb[6][0:9, 0:512], bgt_col,
                                              None, AluOpType.add),
             reads=[bpb[6], bbgt], writes=[bgsb, brow2])
        for u in range(4):
            u0 = u * 128
            P.op("pe", lambda e, _u0=u0: e.transpose(
                pb[4][:, 0:9], gsb[:, _u0:_u0 + 128], identf[0:9, 0:9]),
                reads=[bgsb, bidf], writes=[bpb[4]])
            P.op("dve", lambda e: e.reduce_max(g_negmax, pb[4][:, 0:8], axis=AX.X,
                                               negate=True),
                 reads=[bpb[4]], writes=[bgate])
            P.op("act", lambda e: e.activation(g_pg, pb[4][:, 0:8], AF.Exp,
                                               bias=g_negmax),
                 reads=[bpb[4], bgate], writes=[bgate])
            P.op("dve", lambda e: e.tensor_reduce(g_zi, g_pg, AX.X, AluOpType.add),
                 reads=[bgate], writes=[bgate])
            P.op("dve", lambda e: e.reciprocal(g_zi, g_zi),
                 reads=[bgate], writes=[bgate])
            P.op("dve", lambda e: e.tensor_scalar(g_pg, g_pg, g_zi, None,
                                                  AluOpType.mult),
                 reads=[bgate], writes=[bgate])
            P.op("act", lambda e: e.activation(g_thr, pb[4][:, 8:9], AF.Tanh,
                                               scale=0.5),
                 reads=[bpb[4]], writes=[bgate])
            P.op("dve", lambda e: e.tensor_scalar(g_thr, g_thr, 0.0625, 0.0625,
                                                  AluOpType.mult, AluOpType.add),
                 reads=[bgate], writes=[bgate])
            P.op("dve", lambda e: e.tensor_scalar(g_w, g_pg, g_thr, 0.0,
                                                  AluOpType.subtract,
                                                  AluOpType.max),
                 reads=[bgate], writes=[bgate])
            P.op("dve", lambda e: e.tensor_reduce(g_s, g_w, AX.X, AluOpType.add),
                 reads=[bgate], writes=[bgate])
            P.op("dve", lambda e: e.tensor_scalar(g_eq, g_s, 0.0, None,
                                                  AluOpType.is_equal),
                 reads=[bgate], writes=[bgate])
            P.op("dve", lambda e: e.tensor_tensor(g_s, g_s, g_eq, AluOpType.add),
                 reads=[bgate], writes=[bgate])
            P.op("dve", lambda e: e.reciprocal(g_s, g_s),
                 reads=[bgate], writes=[bgate])
            P.op("dve", lambda e: e.tensor_scalar(g_w, g_w, g_s, None,
                                                  AluOpType.mult),
                 reads=[bgate], writes=[bgate])
            P.op("pe", lambda e: e.transpose(pb[5][0:8, 0:128], g_w, identf),
                 reads=[bgate, bidf], writes=[bpb[5]])
            P.op("act", lambda e, _c0=c0, _u0=u0: e.activation(
                wfm[:, _c0 + _u0:_c0 + _u0 + 128], pb[5][0:8, 0:128], AF.Copy),
                reads=[bpb[5]], writes=[bwfm, bosc2])

    if stages >= 4:
        emit_gate(0)
        emit_gate(1)

    for e_ in range(E if stages >= 5 else 0):
        for ht in range(2):
            t0 = ht * 1024
            for fp2 in range(FC // 2):
                wpar = fp2 % 2
                P.op("wb1a" if wpar == 0 else "wb1b",
                     lambda e, _e=e_, _fp=fp2, _wp=wpar: e.dma_start(
                         out=wb1[_wp][:].rearrange("p (f c) -> p f c", f=2),
                         in_=w1t[_e, 2 * _fp:2 * _fp + 2].rearrange(
                             "f p c -> p f c")),
                     writes=[bwb1[wpar]])
                for fo in range(2):
                    f = 2 * fp2 + fo
                    for tt in range(2):
                        hpar = [0, 1, 6][(2 * f + tt) % 3]
                        for k in range(DC):
                            P.op("pe", lambda e, _fo=fo, _tt=tt, _k=k, _wp=wpar,
                                 _hp=hpar, _t0=t0: e.matmul(
                                     pb[_hp][:, 0:512],
                                     wb1[_wp][:, _fo * 1024 + _k * 128:
                                              _fo * 1024 + (_k + 1) * 128],
                                     colA_v(_k, _t0 + _tt * 512, 512),
                                     start=(_k == 0), stop=(_k == DC - 1)),
                                 reads=[bwb1[wpar], bcolA[k]], writes=[bpb[hpar]])
                        P.op("act", lambda e, _e=e_, _f=f, _tt=tt, _hp=hpar:
                             e.activation(
                                 hp_v(_f, _tt * 512, 512), pb[_hp][:, 0:512],
                                 AF.Gelu_apprx_tanh,
                                 bias=b1sb[:, _e * FC + _f:_e * FC + _f + 1]),
                             reads=[bpb[hpar], bb1] + (hp_alias[f]
                                                       if _ffn_first(e_, ht)
                                                       else []),
                             writes=[bhp[f]])
            if e_ == 0 and ht == 1:
                emit_gate(2)
                emit_gate(3)
            for tt in range(2):
                wr = fsc[64:65, 0:512] if tt == 0 else fsc[64:65, 512:1024]
                P.op("spm", lambda e, _e=e_, _t0=t0, _tt=tt, _wr=wr: e.dma_start(
                    out=_wr,
                    in_=wfm[_e:_e + 1,
                            _t0 + _tt * 512:_t0 + _tt * 512 + 512]),
                    reads=[bwfm], writes=[brow])
                P.op("pe", lambda e, _tt=tt, _wr=wr: e.matmul(
                    pb[4 + _tt][:, 0:512], onesblk[64:65, :], _wr,
                    start=True, stop=True),
                    reads=[brow, bonesf], writes=[bpb[4 + tt]])
                P.op("act", lambda e, _tt=tt: e.activation(
                    rep_v(_tt), pb[4 + _tt][:, 0:512], AF.Copy),
                    reads=[bpb[4 + tt]], writes=[brep[tt], brow2])
            for dd in range(DC):
                for fh in range(2):
                    wpar = fh
                    fam = "wb2a" if wpar == 0 else "wb2b"
                    P.op(fam, lambda e, _e=e_, _dd=dd, _fh=fh, _wp=wpar:
                         e.dma_start(
                             out=wb2[_wp][:],
                             in_=w2t[_e, _dd][:, _fh * 2048:(_fh + 1) * 2048]),
                         writes=[bwb2[wpar]])
                    for tt in range(2):
                        epar = 2 + (2 * dd + tt) % 4
                        for fc2 in range(16):
                            f = fh * 16 + fc2
                            P.op("pe", lambda e, _f=f, _fc=fc2, _wp=wpar, _tt=tt,
                                 _ep=epar, _t0=t0: e.matmul(
                                     pb[_ep][:, 0:512],
                                     wb2[_wp][:, _fc * 128:(_fc + 1) * 128],
                                     hp_v(_f, _tt * 512, 512),
                                     start=(_f == 0), stop=(_f == FC - 1)),
                                 reads=[bwb2[wpar], bhp[f]], writes=[bpb[epar]])
                for tt in range(2):
                    epar = 2 + (2 * dd + tt) % 4
                    P.op("dve", lambda e, _e=e_, _dd=dd, _tt=tt, _ep=epar:
                         e.scalar_tensor_tensor(
                             t1_v(512), pb[_ep][:, 0:512],
                             b2sb[:, _e * DC + _dd:_e * DC + _dd + 1],
                             rep_v(_tt), AluOpType.add, AluOpType.mult),
                         reads=[bpb[epar], bb2, brep[tt]], writes=[bt1])
                    P.op("dve", lambda e, _dd=dd, _tt=tt, _t0=t0: e.tensor_tensor(
                        xs[_dd][:, _t0 + _tt * 512:_t0 + _tt * 512 + 512],
                        xs[_dd][:, _t0 + _tt * 512:_t0 + _tt * 512 + 512],
                        t1_v(512), AluOpType.add),
                        reads=[bt1], writes=[bxs[dd]])

    for c in range(DC):
        P.op("st", lambda e, _c=c: e.dma_start(out=yT[_c * 128:(_c + 1) * 128, :],
                                               in_=xs[_c][:]),
             reads=[bxs[c]])
    if "kvf" in dbg_aps:
        P.op("st", lambda e: e.dma_start(out=dbg_aps["kvf"], in_=kvf[:]),
             reads=[bkvf])
    if "kvk" in dbg_aps:
        P.op("st", lambda e: e.dma_start(out=dbg_aps["kvk"],
                                         in_=AR[:, KVK0:KVK0 + KC * 65]),
             reads=[bkvk])
    if "xn2" in dbg_aps:
        P.op("st", lambda e: e.dma_start(out=dbg_aps["xn2"], in_=AR[:, 0:16384]),
             reads=list(bcolA))
    if "wfm" in dbg_aps:
        P.op("st", lambda e: e.dma_start(out=dbg_aps["wfm"], in_=wfm),
             reads=[bwfm])

    P.final_wait("sp", "st", P.cnt["st"])

    if not emit:
        stack.close()
        return P
    sem_stack = contextlib.ExitStack()
    sems = {}
    for name in ALL_ENGS:
        sems[name] = sem_stack.enter_context(nc.semaphore("s_" + name))
    with nc.Block() as block:

        @block.tensor
        def _(t):
            for fn in P.ops["pe"]:
                fn(t, sems)

        @block.scalar
        def _(t):
            for fn in P.ops["act"]:
                fn(t, sems)

        @block.vector
        def _(t):
            for fn in P.ops["dve"]:
                fn(t, sems)

        @block.gpsimd
        def _(t):
            for fn in P.ops["gp"]:
                fn(t, sems)

        @block.sync
        def _(t):
            for fn in P.ops["sp"]:
                fn(t, sems)

    sem_stack.close()
    stack.close()
    return P


def _ffn_first(e_, ht):
    return e_ == 0 and ht == 0


def build2(nc, dbg=(), stages=6):
    tmp = bass.Bass("TRN2", target_bir_lowering=False)
    P1 = build(tmp, dbg, stages, emit=False)
    return build(nc, dbg, stages, needed=P1.needed)


def _host_layout(inputs):
    x = np.asarray(inputs["L0_attn_x"], dtype=np.float32)
    Wq = np.asarray(inputs["L0_attn_Wq"], dtype=np.float32)
    Wkv = np.asarray(inputs["L0_attn_Wkv"], dtype=np.float32)
    null_kv = np.asarray(inputs["L0_attn_null_kv"], dtype=np.float32)
    Wout = np.asarray(inputs["L0_attn_Wout"], dtype=np.float32)
    Wg = np.asarray(inputs["L0_ffn_Wg"], dtype=np.float32)
    bg = np.asarray(inputs["L0_ffn_bg"], dtype=np.float32)
    Wt = np.asarray(inputs["L0_ffn_Wt"], dtype=np.float32)
    bt = np.asarray(inputs["L0_ffn_bt"], dtype=np.float32)
    W1 = np.asarray(inputs["L0_ffn_W1"], dtype=np.float32)
    b1 = np.asarray(inputs["L0_ffn_b1"], dtype=np.float32)
    W2 = np.asarray(inputs["L0_ffn_W2"], dtype=np.float32)
    b2 = np.asarray(inputs["L0_ffn_b2"], dtype=np.float32)

    shared = {
        "wq": np.ascontiguousarray(Wq),
        "wkv": np.ascontiguousarray(Wkv),
        "nullkv": np.ascontiguousarray(null_kv.reshape(DH, 1)),
        "woutH": np.ascontiguousarray(
            Wout.reshape(H, DH, D).transpose(1, 0, 2).reshape(DH, H * D)),
        "wgt": np.ascontiguousarray(np.concatenate([Wg, Wt], axis=1)),
        "bgt": np.ascontiguousarray(np.concatenate([bg, bt]).reshape(9, 1)),
        "w1t": np.ascontiguousarray(
            W1.reshape(E, 8, 128, FC, 128).transpose(0, 3, 2, 1, 4)
            .reshape(E, FC, 128, 1024)),
        "b1t": np.ascontiguousarray(
            b1.reshape(E, FC, 128).transpose(2, 0, 1).reshape(128, E * FC)),
        "w2t": np.ascontiguousarray(
            W2.reshape(E, FC, 128, DC, 128).transpose(0, 3, 2, 1, 4)
            .reshape(E, DC, 128, 4096)),
        "b2t": np.ascontiguousarray(
            b2.reshape(E, DC, 128).transpose(2, 0, 1).reshape(128, E * DC)),
    }
    jj = np.arange(128)[:, None]
    ii = np.arange(SL)[None, :]
    in_maps = []
    own_cols = []
    xTb_cache = {}
    for core in range(8):
        b = core % 4
        g = core // 4
        if b not in xTb_cache:
            xTb_cache[b] = np.ascontiguousarray(x[b].T)
        xTb = xTb_cache[b]
        cols = (np.arange(NSLOT)[:, None] * 512 + g * 256
                + np.arange(SL)[None, :]).reshape(-1)
        own_cols.append((b, cols))
        m = np.zeros((128, 1280), dtype=BF16NP)
        for r in range(5):
            mr = (jj <= ii + 1 + 256 * g - 128 * r).astype(BF16NP)
            m[:, r * SL:(r + 1) * SL] = mr
        im = dict(shared)
        im["xT"] = xTb
        im["xTo"] = np.ascontiguousarray(xTb[:, cols])
        im["masks"] = m
        in_maps.append(im)
    return in_maps, own_cols


def kernel(**inputs):
    in_maps, own_cols = _host_layout(inputs)
    nc = bass.Bass("TRN2", target_bir_lowering=False)
    build2(nc)
    res = run_bass_kernel_spmd(nc, in_maps, core_ids=list(range(8)))
    out = np.empty((B, N, D), dtype=np.float32)
    for core in range(8):
        b, cols = own_cols[core]
        out[b, cols, :] = res.results[core]["yT"].T
    return out
